# version 02
# baseline (speedup 1.0000x reference)
import numpy as np
import concourse.bass as bass
import concourse.mybir as mybir
from concourse.bass_utils import run_bass_kernel_spmd

F32 = mybir.dt.float32
AF = mybir.ActivationFunctionType
ALU = mybir.AluOpType
AX = mybir.AxisListType

D = 2048
NTOK = 2048
NQ = 4
QT = 512
QH = 528
NHP = 16
HD = 64
EPS = 1e-5
NEG = -1e30
EXPERTS = 16
MOE_SCALE = 2.5

ENGINES = ("sync", "gpsimd", "tensor", "vector", "scalar")


def _prepack(inp):
    f = np.float32
    qkv_w = np.asarray(inp["L0_attn_qkv_w"], f)
    proj_w = np.asarray(inp["L0_attn_proj_w"], f)
    proj_b = np.asarray(inp["L0_attn_proj_b"], f)
    sw1 = np.asarray(inp["L0_ffn_sw1"], f)
    swg = np.asarray(inp["L0_ffn_swg"], f)
    sw2 = np.asarray(inp["L0_ffn_sw2"], f)
    rw1 = np.asarray(inp["L0_ffn_rw1"], f)
    rwg = np.asarray(inp["L0_ffn_rwg"], f)
    rw2 = np.asarray(inp["L0_ffn_rw2"], f)
    Wg = np.asarray(inp["L0_ffn_Wg"], f)
    Wdown = np.asarray(inp["L0_ffn_Wdown"], f)
    Wup = np.asarray(inp["L0_ffn_Wup"], f)
    biases = np.asarray(inp["L0_ffn_biases"], f)

    w = {}
    t = qkv_w.reshape(3, 16, 128, 16, 128)
    w["qkvw"] = np.ascontiguousarray(t.transpose(1, 0, 4, 3, 2))
    t = proj_w.reshape(16, 128, 16, 128)
    w["pj"] = np.ascontiguousarray(t.transpose(0, 3, 2, 1))
    w["pjb"] = np.ascontiguousarray(proj_b.reshape(1, 2048))
    a = sw1.reshape(8, 128, 16, 128).transpose(0, 3, 2, 1)
    b = swg.reshape(8, 128, 16, 128).transpose(0, 3, 2, 1)
    w["sw1g"] = np.ascontiguousarray(np.stack([a, b], axis=1))
    w["wdn"] = np.ascontiguousarray(
        Wdown.reshape(4, 128, 16, 128).transpose(0, 3, 2, 1))
    w["wg"] = np.ascontiguousarray(Wg.reshape(16, 16, 128).transpose(2, 1, 0))
    a = rw1.reshape(16, 8, 128, 4, 128).transpose(0, 1, 4, 3, 2)
    b = rwg.reshape(16, 8, 128, 4, 128).transpose(0, 1, 4, 3, 2)
    w["rw1g"] = np.ascontiguousarray(np.stack([a, b], axis=2))
    w["rw2"] = np.ascontiguousarray(
        rw2.reshape(16, 512, 8, 128).transpose(0, 3, 2, 1))
    w["sw2"] = np.ascontiguousarray(
        sw2.reshape(8, 256, 8, 128).transpose(0, 3, 2, 1))
    w["wup"] = np.ascontiguousarray(
        Wup.reshape(8, 256, 4, 128).transpose(0, 3, 2, 1))
    w["bias_bc"] = np.ascontiguousarray(np.broadcast_to(biases, (128, 16)))

    p_i = np.arange(128)[:, None]
    c_i = np.arange(143)[None, :]
    o = c_i - p_i - 8
    band = (o >= -8) & (o < 8)
    mval = (8.0 - np.abs(o) / 2.0) / 8.0
    m2, bb = [], []
    for v in range(3):
        valid = band.copy()
        if v == 0:
            valid &= (c_i - 8 >= 0)
        if v == 2:
            valid &= (c_i - 8 < 128)
        m2.append(np.where(valid, mval, 1.0).astype(f))
        bb.append(np.where(valid, 0.0, NEG).astype(f))
    w["maskM"] = np.ascontiguousarray(np.stack(m2, axis=1))
    w["maskB"] = np.ascontiguousarray(np.stack(bb, axis=1))
    w["ident"] = np.eye(128, dtype=f)
    w["onesrow"] = np.ones((1, 128), dtype=f)
    w["consts"] = np.ascontiguousarray(
        np.broadcast_to(np.array([-1.0, EPS], f), (128, 2)))
    return w


class MK:
    def __init__(self, nc, taps=()):
        self.nc = nc
        self.taps = set(taps)
        self.dbg = {}
        self.ops = {e: [] for e in ENGINES}
        self.cnt = {s: 0 for s in ("ld", "lg", "st", "mm", "ev", "es")}
        self.waited = {}
        self.sems = {}
        self.vw = set()
        self.vr = set()
        self.bankg = [None] * 8

    def emit(self, eng, fn):
        self.ops[eng].append(fn)

    def wait(self, eng, sem, val):
        if not val or val <= 0:
            return
        k = (eng, sem)
        if self.waited.get(k, 0) >= val:
            return
        self.waited[k] = val
        self.emit(eng, lambda E, s=sem, v=val: E.wait_ge(self.sems[s], v))

    def dma(self, eng, sem, out_ap, in_ap):
        self.cnt[sem] += 16
        v = self.cnt[sem]
        self.emit(eng, lambda E, o=out_ap, i=in_ap, s=sem:
                  E.dma_start(out=o, in_=i).then_inc(self.sems[s], 16))
        return v

    def top(self, fn, inc=True):
        if inc:
            self.cnt["mm"] += 1
            self.emit("tensor", lambda E, f=fn: f(E).then_inc(self.sems["mm"], 1))
            return self.cnt["mm"]
        self.emit("tensor", fn)
        return None

    def vop(self, fn, reads=(), writes=(), inc=True):
        r, w = set(reads), set(writes)
        if (r & self.vw) or (w & self.vr) or (w & self.vw):
            self.emit("vector", lambda E: E.drain())
        self.vr, self.vw = r, w
        if inc:
            self.cnt["ev"] += 1
            self.emit("vector", lambda E, f=fn: f(E).then_inc(self.sems["ev"], 1))
            return self.cnt["ev"]
        self.emit("vector", fn)
        return None

    def sop(self, fn, inc=True):
        if inc:
            self.cnt["es"] += 1
            self.emit("scalar", lambda E, f=fn: f(E).then_inc(self.sems["es"], 1))
            return self.cnt["es"]
        self.emit("scalar", fn)
        return None

    def use_bank(self, b):
        g = self.bankg[b]
        if g:
            self.wait("tensor", g[0], g[1])

    def bank_done(self, b, sem, val):
        self.bankg[b] = (sem, val)

    def fence(self, eng):
        for s in ("mm", "ev", "es"):
            self.wait(eng, s, self.cnt[s])

    def tap(self, name, ap, shape):
        if name not in self.taps:
            return
        d = self.nc.dram_tensor("dbg_" + name, list(shape), F32,
                                kind="ExternalOutput").ap()
        self.dbg[name] = d
        for s in ("mm", "ev", "es"):
            self.wait("gpsimd", s, self.cnt[s])
        self.dma("gpsimd", "st", d, ap)


def build(taps=(), nq=NQ):
    nc = bass.Bass("TRN2", target_bir_lowering=False)
    mk = MK(nc, taps)
    f32 = F32

    X = nc.dram_tensor("x", [NTOK, D], f32, kind="ExternalInput").ap()
    OUT = nc.dram_tensor("out", [NTOK, D], f32, kind="ExternalOutput").ap()
    H = {}
    for n, s in [("qkvw", [16, 3, 128, 16, 128]), ("pj", [16, 128, 16, 128]),
                 ("pjb", [1, 2048]), ("sw1g", [8, 2, 128, 16, 128]),
                 ("wdn", [4, 128, 16, 128]), ("wg", [128, 16, 16]),
                 ("rw1g", [16, 8, 2, 128, 4, 128]), ("rw2", [16, 128, 8, 512]),
                 ("sw2", [8, 128, 8, 256]), ("wup", [8, 128, 4, 256]),
                 ("bias_bc", [128, 16]), ("maskM", [128, 3, 143]),
                 ("maskB", [128, 3, 143]), ("ident", [128, 128]),
                 ("onesrow", [1, 128]), ("consts", [128, 2])]:
        H[n] = nc.dram_tensor(n, s, f32, kind="ExternalInput").ap()

    R1 = nc.alloc_sbuf_tensor("R1", [128, 16 * QH], f32).ap()
    R2 = nc.alloc_sbuf_tensor("R2", [128, 16 * 512], f32).ap()
    R34 = nc.alloc_sbuf_tensor("R34", [128, 4 * 2048], f32).ap()
    R7 = nc.alloc_sbuf_tensor("R7", [128, 12288], f32).ap()
    LNB = nc.alloc_sbuf_tensor("LNB", [128, 2048], f32).ap()
    STT = nc.alloc_sbuf_tensor("STT", [128, 192], f32).ap()
    M8S = nc.alloc_sbuf_tensor("M8S", [128, 32], f32).ap()
    MM2 = nc.alloc_sbuf_tensor("MM2", [128, 3 * 143], f32).ap()
    MB2 = nc.alloc_sbuf_tensor("MB2", [128, 3 * 143], f32).ap()
    IDT = nc.alloc_sbuf_tensor("IDT", [128, 128], f32).ap()
    CST = nc.alloc_sbuf_tensor("CST", [128, 2], f32).ap()
    ONE = nc.alloc_sbuf_tensor("ONE", [1, 128], f32).ap()
    PJB = nc.alloc_sbuf_tensor("PJB", [1, 2048], f32).ap()
    BIA = nc.alloc_sbuf_tensor("BIA", [128, 16], f32).ap()
    WDN = nc.alloc_sbuf_tensor("WDS", [128, 64], f32).ap()
    OUTB = nc.alloc_sbuf_tensor("OUTB", [128, 2 * 2048], f32).ap()
    RW2B = nc.alloc_sbuf_tensor("RW2B", [128, 8 * 512], f32).ap()
    nc.const_aps.register(f32, -1.0, CST[:, 0:1])
    nc.const_aps.register(f32, EPS, CST[:, 1:2])

    pb = [nc.alloc_psum_tensor(f"pb{i}", [128, 512], f32).ap() for i in range(8)]

    def r7(a, b):
        return R7[:, a:b]

    def ln1T(c):
        return R1[:, c * QH:(c + 1) * QH]

    def ffn_mt(mt):
        return R1[:, mt * 2048:(mt + 1) * 2048]

    def aoT(hp):
        return R2[:, hp * 512:(hp + 1) * 512]

    def hidT(mh):
        return R2[:, mh * 512:(mh + 1) * 512]

    def hidET(mh):
        return R2[:, 4096 + mh * 512:4096 + (mh + 1) * 512]

    def projh(mt):
        return R34[:, mt * 2048:(mt + 1) * 2048]

    def ln3T_blk(mt, c):
        return R34[:, mt * 2048 + c * 128:mt * 2048 + (c + 1) * 128]



    st_col = {}

    def sc(name):
        if name not in st_col:
            st_col[name] = len(st_col)
            assert len(st_col) <= 192, name
        i = st_col[name]
        return STT[:, i:i + 1]

    ld0 = 0
    for dst, src in [(MM2, H["maskM"]), (MB2, H["maskB"]), (IDT, H["ident"]),
                     (CST, H["consts"]), (ONE, H["onesrow"]), (PJB, H["pjb"]),
                     (BIA, H["bias_bc"])]:
        ld0 = mk.dma("sync", "ld", dst, src)
    for e in ("vector", "scalar", "tensor"):
        mk.wait(e, "ld", ld0)

    guard = {}

    def buf_guard(key, eng):
        g = guard.get(key)
        if g:
            mk.wait(eng, g[0], g[1])

    def set_guard(key, sem, val):
        guard[key] = (sem, val)

    def ln_math(pre, cols, widths=None):
        widths = widths or [128] * len(cols)
        for m, w in zip(cols, widths):
            mk.vop(lambda E, m_=m, w_=w:
                   E.tensor_scalar_mul(sc(f"{pre}_mu{m_}")[0:w_],
                                       sc(f"{pre}_s2{m_}")[0:w_], 1.0 / D),
                   reads=[f"{pre}_s2{m}"], writes=[f"{pre}_mu{m}"])
        for m, w in zip(cols, widths):
            mk.vop(lambda E, m_=m, w_=w:
                   E.tensor_scalar_mul(sc(f"{pre}_msq{m_}")[0:w_],
                                       sc(f"{pre}_q2{m_}")[0:w_], 1.0 / D),
                   reads=[f"{pre}_q2{m}"], writes=[f"{pre}_msq{m}"])
        for m, w in zip(cols, widths):
            mk.vop(lambda E, m_=m, w_=w:
                   E.scalar_tensor_tensor(sc(f"{pre}_nv{m_}")[0:w_],
                                          sc(f"{pre}_mu{m_}")[0:w_],
                                          sc(f"{pre}_mu{m_}")[0:w_],
                                          sc(f"{pre}_msq{m_}")[0:w_],
                                          op0=ALU.mult, op1=ALU.subtract),
                   reads=[f"{pre}_mu{m}", f"{pre}_msq{m}"],
                   writes=[f"{pre}_nv{m}"])
        mk.wait("scalar", "ev", mk.cnt["ev"])
        for m, w in zip(cols, widths):
            mk.sop(lambda E, m_=m, w_=w:
                   E.activation(sc(f"{pre}_std{m_}")[0:w_],
                                sc(f"{pre}_nv{m_}")[0:w_],
                                AF.Sqrt, bias=EPS, scale=-1.0))
        mk.wait("vector", "es", mk.cnt["es"])
        for m, w in zip(cols, widths):
            mk.vop(lambda E, m_=m, w_=w:
                   E.reciprocal(sc(f"{pre}_rstd{m_}")[0:w_],
                                sc(f"{pre}_std{m_}")[0:w_]),
                   reads=[f"{pre}_std{m}"], writes=[f"{pre}_rstd{m}"])
        for m, w in zip(cols, widths):
            mk.vop(lambda E, m_=m, w_=w:
                   E.scalar_tensor_tensor(sc(f"{pre}_nmr{m_}")[0:w_],
                                          sc(f"{pre}_mu{m_}")[0:w_],
                                          -1.0, sc(f"{pre}_rstd{m_}")[0:w_],
                                          op0=ALU.mult, op1=ALU.mult),
                   reads=[f"{pre}_mu{m}", f"{pre}_rstd{m}"],
                   writes=[f"{pre}_nmr{m}"])
        return mk.cnt["ev"]

    for q in range(nq):
        g0 = q * QT - 8
        mk.fence("gpsimd")

        xb = [r7(2048 * i, 2048 * (i + 1)) for i in range(5)]
        widths = [128, 128, 128, 128, 16]
        stA = []
        for t in range(5):
            w = widths[t]
            r0 = g0 + 128 * t
            cur = t
            g = guard.get(("xb", cur))
            if g:
                mk.wait("gpsimd", g[0], g[1])
            lo, hi = max(r0, 0), min(r0 + w, NTOK)
            p0 = lo - r0
            if p0 > 0 or hi - r0 < w:
                if g:
                    mk.wait("vector", g[0], g[1])
                mvz = mk.vop(lambda E, b=xb[cur], a=p0, ww=w:
                             (E.memset(b[0:a, :], 0.0) if a > 0 else
                              E.memset(b[0:ww, :], 0.0)),
                             writes=[("xb", cur)])
                mk.wait("gpsimd", "ev", mvz)
            lg = mk.dma("gpsimd", "lg", xb[cur][p0:p0 + (hi - lo), :], X[lo:hi, :])
            mk.wait("scalar", "lg", lg)
            mk.wait("vector", "lg", lg)
            es1 = mk.sop(lambda E, b=xb[cur], w_=w, t_=t:
                         E.activation(LNB[0:w_, :], b[0:w_, :], AF.Square,
                                      accum_out=sc(f"A_q2{t_}")[0:w_]))
            mk.vop(lambda E, b=xb[cur], w_=w, t_=t:
                   E.reduce_sum(sc(f"A_s2{t_}")[0:w_], b[0:w_, :], axis=AX.X),
                   reads=[("xb", cur)], writes=[f"A_s2{t}"])
            stA.append((t, w, cur, es1))
        mk.wait("vector", "es", mk.cnt["es"])
        ln_math("A", [t for t, _, _, _ in stA], [w for _, w, _, _ in stA])
        mk.wait("scalar", "ev", mk.cnt["ev"])
        mm_prev_t = 0
        for t, w, cur, _ in stA:
            mk.wait("scalar", "mm", mm_prev_t)
            esap = mk.sop(lambda E, b=xb[cur], w_=w, t_=t:
                          E.activation(LNB[0:w_, :], b[0:w_, :], AF.Identity,
                                       scale=sc(f"A_rstd{t_}")[0:w_],
                                       bias=sc(f"A_nmr{t_}")[0:w_]))
            set_guard(("xb", cur), "es", esap)
            mk.wait("tensor", "es", esap)
            for rnd in range(4):
                bk = 6 + (rnd % 2)
                mk.use_bank(bk)
                for j in range(4):
                    c = rnd * 4 + j
                    mk.top(lambda E, w_=w, c_=c, j_=j, b_=bk:
                           E.transpose(pb[b_][0:128, j_ * 128:j_ * 128 + w_],
                                       LNB[0:w_, c_ * 128:(c_ + 1) * 128],
                                       IDT[0:w_, 0:w_]),
                           inc=(j == 3))
                mmv = mk.cnt["mm"]
                mk.wait("vector", "mm", mmv)
                evv = None
                for j in range(4):
                    c = rnd * 4 + j
                    evv = mk.vop(lambda E, c_=c, j_=j, b_=bk, w_=w, t_=t:
                                 E.tensor_copy(
                                     R1[:, c_ * QH + 128 * t_:c_ * QH + 128 * t_ + w_],
                                     pb[b_][:, 128 * j_:128 * j_ + w_]),
                                 writes=["ln1T"])
                mk.bank_done(bk, "ev", evv)
            mm_prev_t = mk.cnt["mm"]
        ln1_ready = mk.cnt["ev"]
        mk.tap(f"q{q}_ln1T", R1[:, 0:16 * QH], [128, 16 * QH])

        mk.fence("sync")
        wq = [r7(0, 2048), r7(2048, 4096)]
        qkvT = r7(4096, 4096 + 3 * QH)

        def qv(part):
            return qkvT[:, part * QH:(part + 1) * QH]
        vtok = r7(5680, 6000)
        sct = r7(6000, 6143)
        pex = r7(6144, 6287)
        atT = r7(6288, 6544)
        mk.wait("tensor", "ev", ln1_ready)

        for hp in range(NHP):
            evq = [0, 0, 0]
            for part in range(3):
                cur = (3 * hp + part) % 2
                buf_guard(("wq", cur), "sync")
                lw = mk.dma("sync", "ld", wq[cur], H["qkvw"][hp, part])
                mk.wait("tensor", "ld", lw)
                for sl in range(2):
                    bk = (2 * part + sl) % 4
                    mk.use_bank(bk)
                    for k in range(16):
                        mk.top(lambda E, c=cur, k_=k, s=sl, b_=bk:
                               E.matmul(pb[b_][:, 0:264],
                                        wq[c][:, k_ * 128:(k_ + 1) * 128],
                                        ln1T(k_)[:, s * 264:s * 264 + 264],
                                        start=(k_ == 0), stop=(k_ == 15)),
                               inc=(k == 15))
                    mmv = mk.cnt["mm"]
                    mk.wait("vector", "mm", mmv)
                    if part == 0:
                        evv = mk.vop(lambda E, s=sl, b_=bk:
                                     E.tensor_scalar_mul(
                                         qkvT[:, s * 264:s * 264 + 264],
                                         pb[b_][:, 0:264], HD ** -0.5),
                                     writes=["qkvT"])
                    else:
                        evv = mk.vop(lambda E, p_=part, s=sl, b_=bk:
                                     E.tensor_copy(qv(p_)[:, s * 264:s * 264 + 264],
                                                   pb[b_][:, 0:264]),
                                     writes=["qkvT"])
                    mk.bank_done(bk, "ev", evv)
                set_guard(("wq", cur), "mm", mk.cnt["mm"])
                evq[part] = mk.cnt["ev"]
            if hp == 0:
                mk.tap(f"q{q}_qkvT", qkvT, [128, 3 * QH])

            for h01 in range(2):
                hb = 64 * h01
                mk.wait("tensor", "ev", evq[2])
                mk.use_bank(4)
                for v5 in range(5):
                    wv = 128 if v5 < 4 else 16
                    mk.top(lambda E, v_=v5, w_=wv, h_=hb:
                           E.transpose(pb[4][0:w_, v_ * 64:v_ * 64 + 64],
                                       qv(2)[h_:h_ + 64, v_ * 128:v_ * 128 + w_],
                                       IDT[h_:h_ + 64, h_:h_ + 64]),
                           inc=(v5 == 4))
                mmv = mk.cnt["mm"]
                mk.wait("vector", "mm", mmv)
                ev_vt = mk.vop(lambda E: E.tensor_copy(vtok, pb[4][:, 0:320]),
                               writes=["vtok"])
                mk.bank_done(4, "ev", ev_vt)

                def emit_qk(b):
                    gq = 4 * q + b
                    var = 0 if gq == 0 else (2 if gq == 15 else 1)
                    bk = 4 + (b % 2)
                    mk.use_bank(bk)
                    mk.top(lambda E, b_=b, h_=hb, k_=bk:
                           E.matmul(pb[k_][:, 0:143],
                                    qv(0)[h_:h_ + 64, 8 + 128 * b_:136 + 128 * b_],
                                    qv(1)[h_:h_ + 64, 128 * b_:128 * b_ + 143],
                                    start=True, stop=False), inc=False)
                    mmv = mk.top(lambda E, v_=var, k_=bk:
                                 E.matmul(pb[k_][:, 0:143], IDT,
                                          MB2[:, v_ * 143:(v_ + 1) * 143],
                                          start=False, stop=True))
                    return b, var, bk, mmv

                def emit_tail(u):
                    b, var, bk, mmqk = u
                    mk.wait("vector", "mm", mmqk)
                    mk.vop(lambda E, k_=bk:
                           E.reduce_max(sc("B_nrm"), pb[k_][:, 0:143],
                                        axis=AX.X, negate=True),
                           writes=["B_nrm"])
                    ev_sc = mk.vop(lambda E, v_=var, k_=bk:
                                   E.tensor_tensor(sct, pb[k_][:, 0:143],
                                                   MM2[:, v_ * 143:(v_ + 1) * 143],
                                                   op=ALU.mult),
                                   writes=["sct"])
                    mk.bank_done(bk, "ev", ev_sc)
                    mk.wait("scalar", "ev", ev_sc)
                    es_e = mk.sop(lambda E:
                                  E.activation(pex, sct, AF.Exp, bias=sc("B_nrm"),
                                               accum_out=sc("B_rsum")))
                    mk.wait("vector", "es", es_e)
                    ev_ri = mk.vop(lambda E:
                                   E.reciprocal(sc("B_rinv"), sc("B_rsum")),
                                   reads=["B_rsum"], writes=["B_rinv"])
                    mk.wait("scalar", "ev", ev_ri)
                    es_pn = mk.sop(lambda E: E.mul(sct, pex, sc("B_rinv")))
                    mk.wait("tensor", "es", es_pn)
                    mk.use_bank(6)
                    mk.top(lambda E: E.transpose(pb[6][0:128, 0:128],
                                                 sct[:, 0:128], IDT), inc=False)
                    mmT = mk.top(lambda E: E.transpose(pb[6][0:15, 128:256],
                                                       sct[:, 128:143], IDT))
                    mk.wait("vector", "mm", mmT)
                    ev_aT = mk.vop(lambda E: E.tensor_copy(atT, pb[6][:, 0:256]),
                                   writes=["atT"])
                    mk.bank_done(6, "ev", ev_aT)
                    mk.wait("tensor", "ev", ev_aT)
                    mk.use_bank(7)
                    mk.top(lambda E, b_=b:
                           E.matmul(pb[7][0:64, 0:128],
                                    vtok[:, 64 * b_:64 * b_ + 64],
                                    atT[:, 0:128], start=True, stop=False),
                           inc=False)
                    mmAV = mk.top(lambda E, b_=b:
                                  E.matmul(pb[7][0:64, 0:128],
                                           vtok[0:15, 64 * b_ + 64:64 * b_ + 128],
                                           atT[0:15, 128:256],
                                           start=False, stop=True))
                    mk.wait("vector", "mm", mmAV)
                    ev_o = mk.vop(lambda E, b_=b, h_=hb, hp_=hp:
                                  E.tensor_copy(
                                      aoT(hp_)[h_:h_ + 64, 128 * b_:128 * b_ + 128],
                                      pb[7][0:64, 0:128]),
                                  writes=["aoT"])
                    mk.bank_done(7, "ev", ev_o)

                prev = None
                for b in range(4):
                    u = emit_qk(b)
                    if prev is not None:
                        emit_tail(prev)
                    prev = u
                emit_tail(prev)
        ao_ready = mk.cnt["ev"]
        mk.tap(f"q{q}_aoT", R2[:, 0:8192], [128, 8192])

        mk.fence("sync")
        mk.fence("gpsimd")
        wpj = [r7(0, 2048), r7(2048, 4096)]
        xr = [r7(4096, 6144), r7(6144, 8192)]
        mk.wait("tensor", "ev", ao_ready)
        for m in range(4):
            mk.vop(lambda E, m_=m: E.memset(sc(f"C_s2{m_}"), 0.0),
                   writes=[f"C_s2{m}"], inc=False)
            mk.vop(lambda E, m_=m: E.memset(sc(f"C_q2{m_}"), 0.0),
                   writes=[f"C_q2{m}"], inc=False)
        for n16 in range(16):
            cur = n16 % 2
            buf_guard(("wpj", cur), "sync")
            lw = mk.dma("sync", "ld", wpj[cur], H["pj"][n16])
            mk.wait("tensor", "ld", lw)
            for m in range(4):
                bk = m % 4
                mk.use_bank(bk)
                for k in range(16):
                    mk.top(lambda E, c=cur, k_=k, m_=m, b_=bk:
                           E.matmul(pb[b_][:, 0:128],
                                    aoT(k_)[:, 128 * m_:128 * m_ + 128],
                                    wpj[c][:, 128 * k_:128 * k_ + 128],
                                    start=(k_ == 0), stop=False), inc=False)
                mmv = mk.top(lambda E, n_=n16, b_=bk:
                             E.matmul(pb[b_][:, 0:128], ONE,
                                      PJB[0:1, 128 * n_:128 * n_ + 128],
                                      start=False, stop=True))
                mk.wait("scalar", "mm", mmv)
                e1 = mk.sop(lambda E, m_=m, n_=n16, b_=bk:
                            E.activation(projh(m_)[:, 128 * n_:128 * n_ + 128],
                                         pb[b_][:, 0:128], AF.Identity,
                                         accum_out=sc("C_ps")))
                e2 = mk.sop(lambda E, b_=bk:
                            E.activation(LNB[:, 0:128], pb[b_][:, 0:128],
                                         AF.Square, accum_out=sc("C_pq")))
                mk.bank_done(bk, "es", e2)
                mk.wait("vector", "es", e2)
                mk.vop(lambda E, m_=m:
                       E.tensor_add(sc(f"C_s2{m_}"), sc(f"C_s2{m_}"), sc("C_ps")),
                       reads=["C_ps", f"C_s2{m}"], writes=[f"C_s2{m}"])
                mk.vop(lambda E, m_=m:
                       E.tensor_add(sc(f"C_q2{m_}"), sc(f"C_q2{m_}"), sc("C_pq")),
                       reads=["C_pq", f"C_q2{m}"], writes=[f"C_q2{m}"])
            set_guard(("wpj", cur), "mm", mk.cnt["mm"])
        mk.tap(f"q{q}_proj", R34[:, 0:8192], [128, 8192])

        ln_math("C", range(4))
        mk.wait("scalar", "ev", mk.cnt["ev"])
        for m in range(4):
            cur = m % 2
            buf_guard(("xr", cur), "gpsimd")
            r0 = q * QT + 128 * m
            lg = mk.dma("gpsimd", "lg", xr[cur], X[r0:r0 + 128, :])
            mk.wait("scalar", "lg", lg)
            mk.wait("scalar", "ev", mk.cnt["ev"])
            e1 = mk.sop(lambda E, c=cur, m_=m:
                        E.activation(LNB, xr[c], AF.Identity, bias=sc(f"C_nmr{m_}")))
            set_guard(("xr", cur), "es", e1)
            mk.wait("vector", "es", e1)
            evh = mk.vop(lambda E, m_=m:
                         E.scalar_tensor_tensor(projh(m_), projh(m_),
                                                sc(f"C_rstd{m_}"), LNB,
                                                op0=ALU.mult, op1=ALU.add),
                         reads=["LNB", ("projh", m)], writes=[("projh", m)])
            mk.wait("scalar", "ev", evh)
            mk.sop(lambda E, m_=m:
                   E.activation(LNB, projh(m_), AF.Square,
                                accum_out=sc(f"D_q2{m_}")))
            mk.vop(lambda E, m_=m:
                   E.reduce_sum(sc(f"D_s2{m_}"), projh(m_), axis=AX.X),
                   reads=[("projh", m)], writes=[f"D_s2{m}"])
        mk.wait("vector", "es", mk.cnt["es"])
        ln_math("D", range(4))
        mk.wait("scalar", "ev", mk.cnt["ev"])
        mm_prev_t = 0
        for m in range(4):
            mk.wait("scalar", "mm", mm_prev_t)
            e1 = mk.sop(lambda E, m_=m:
                        E.activation(LNB, projh(m_), AF.Identity,
                                     scale=sc(f"D_rstd{m_}"), bias=sc(f"D_nmr{m_}")))
            mk.wait("tensor", "es", e1)
            for rnd in range(4):
                bk = 6 + (rnd % 2)
                mk.use_bank(bk)
                for j in range(4):
                    c = rnd * 4 + j
                    mk.top(lambda E, c_=c, j_=j, b_=bk:
                           E.transpose(pb[b_][0:128, j_ * 128:(j_ + 1) * 128],
                                       LNB[:, c_ * 128:(c_ + 1) * 128], IDT),
                           inc=(j == 3))
                mmv = mk.cnt["mm"]
                mk.wait("vector", "mm", mmv)
                evv = mk.vop(lambda E, m_=m, b_=bk, r_=rnd:
                             E.tensor_copy(
                                 R34[:, m_ * 2048 + r_ * 512:m_ * 2048 + (r_ + 1) * 512],
                                 pb[b_][:, 0:512]),
                             writes=["ln3T"])
                mk.bank_done(bk, "ev", evv)
            mm_prev_t = mk.cnt["mm"]
        ln3_ready = mk.cnt["ev"]
        mk.tap(f"q{q}_ln3T", R34[:, 0:8192], [128, 8192])

        mk.fence("sync")
        ws = [r7(0, 2048), r7(2048, 4096)]
        xlT = r7(4096, 6144)
        accTok = r7(6144, 8192)
        accT = r7(8192, 10240)
        wgb = r7(10240, 10496)
        prT = r7(10496, 11008)
        moe = r7(11008, 11264)
        mk.wait("tensor", "ev", ln3_ready)
        wcur = [0]

        def wslot():
            s = wcur[0] % 2
            wcur[0] += 1
            return s

        prev_evv = 0
        for mh in range(8):
            psl = {}
            for t2 in (1, 0):
                s = wslot()
                buf_guard(("ws", s), "sync")
                lw = mk.dma("sync", "ld", ws[s], H["sw1g"][mh, t2])
                mk.wait("tensor", "ld", lw)
                bk = (2 * mh + t2) % 4
                mk.use_bank(bk)
                for k in range(16):
                    for mt4 in range(4):
                        mk.top(lambda E, s_=s, k_=k, b_=bk, m_=mt4:
                               E.matmul(pb[b_][:, 128 * m_:128 * m_ + 128],
                                        ws[s_][:, 128 * k_:128 * k_ + 128],
                                        ln3T_blk(m_, k_),
                                        start=(k_ == 0 and m_ == 0),
                                        stop=(k_ == 15 and m_ == 3)),
                               inc=(k == 15 and mt4 == 3))
                psl[t2] = (bk, mk.cnt["mm"])
                set_guard(("ws", s), "mm", mk.cnt["mm"])
            bkg, mmg = psl[1]
            bkl, mml = psl[0]
            mk.wait("scalar", "mm", mmg)
            mk.wait("scalar", "ev", prev_evv)
            e1 = mk.sop(lambda E, b_=bkg: E.activation(LNB[:, 0:512], pb[b_], AF.Relu))
            e2 = mk.sop(lambda E: E.activation(LNB[:, 512:1024], LNB[:, 0:512],
                                               AF.Square))
            mk.bank_done(bkg, "es", e1)
            mk.wait("vector", "es", e2)
            mk.wait("vector", "mm", mml)
            evv = mk.vop(lambda E, m_=mh, b_=bkl:
                         E.tensor_tensor(hidT(m_), LNB[:, 512:1024], pb[b_],
                                         op=ALU.mult),
                         reads=["LNB2"], writes=[("hidT", mh)])
            mk.bank_done(bkl, "ev", evv)
            prev_evv = evv
        for ml in range(4):
            s = wslot()
            buf_guard(("ws", s), "sync")
            lw = mk.dma("sync", "ld", ws[s], H["wdn"][ml])
            mk.wait("tensor", "ld", lw)
            bk = ml % 4
            mk.use_bank(bk)
            for k in range(16):
                for mt4 in range(4):
                    mk.top(lambda E, s_=s, k_=k, b_=bk, m_=mt4:
                           E.matmul(pb[b_][:, 128 * m_:128 * m_ + 128],
                                    ws[s_][:, 128 * k_:128 * k_ + 128],
                                    ln3T_blk(m_, k_),
                                    start=(k_ == 0 and m_ == 0),
                                    stop=(k_ == 15 and m_ == 3)),
                           inc=(k == 15 and mt4 == 3))
            mmv = mk.cnt["mm"]
            set_guard(("ws", s), "mm", mmv)
            mk.wait("vector", "mm", mmv)
            evv = mk.vop(lambda E, m_=ml, b_=bk:
                         E.tensor_copy(xlT[:, 512 * m_:512 * m_ + 512], pb[b_]),
                         writes=["xlT"])
            mk.bank_done(bk, "ev", evv)
        mk.tap(f"q{q}_xlT", xlT, [128, 2048])
        lw = mk.dma("sync", "ld", wgb, H["wg"])
        mk.wait("tensor", "ld", lw)
        mk.use_bank(4)
        for k in range(16):
            for mt4 in range(4):
                mk.top(lambda E, k_=k, m_=mt4:
                       E.matmul(pb[4][0:16, 128 * m_:128 * m_ + 128],
                                wgb[:, 16 * k_:16 * k_ + 16],
                                ln3T_blk(m_, k_),
                                start=(k_ == 0 and m_ == 0),
                                stop=(k_ == 15 and m_ == 3)),
                       inc=(k == 15 and mt4 == 3))
        mmv = mk.cnt["mm"]
        mk.wait("vector", "mm", mmv)
        ev_pr = mk.vop(lambda E: E.tensor_copy(prT[0:16, :], pb[4][0:16, 0:512]),
                       writes=["prT"])
        mk.bank_done(4, "ev", ev_pr)
        mk.wait("tensor", "ev", ev_pr)
        mk.use_bank(5)
        for mt in range(4):
            mk.top(lambda E, m_=mt:
                   E.transpose(pb[5][0:128, 16 * m_:16 * m_ + 16],
                               prT[0:16, 128 * m_:128 * m_ + 128],
                               IDT[0:16, 0:16]),
                   inc=(mt == 3))
        mmv = mk.cnt["mm"]
        mk.wait("scalar", "mm", mmv)
        es_sig = mk.sop(lambda E: E.activation(moe[:, 0:64], pb[5][:, 0:64],
                                               AF.Sigmoid))
        mk.bank_done(5, "es", es_sig)
        mk.wait("vector", "es", es_sig)

        def sig_mt(mt):
            return moe[:, 16 * mt:16 * mt + 16]

        def z_mt(mt):
            return moe[:, 64 + 16 * mt:80 + 16 * mt]

        def ch_mt(mt):
            return moe[:, 128 + 16 * mt:144 + 16 * mt]

        for mt in range(4):
            mk.vop(lambda E, m_=mt: E.tensor_add(z_mt(m_), sig_mt(m_), BIA),
                   reads=[("sig", mt)], writes=[("zsc", mt)])
        for mt in range(4):
            mk.vop(lambda E, m_=mt: E.max(M8S[:, 8 * m_:8 * m_ + 8], z_mt(m_)),
                   reads=[("zsc", mt)], writes=[("m8", mt)])
        for mt in range(4):
            mk.vop(lambda E, m_=mt:
                   E.match_replace(ch_mt(m_), in_to_replace=M8S[:, 8 * m_:8 * m_ + 8],
                                   in_values=z_mt(m_), imm_value=NEG),
                   reads=[("m8", mt), ("zsc", mt)], writes=[("chz", mt)])
        for mt in range(4):
            mk.vop(lambda E, m_=mt:
                   E.tensor_tensor(z_mt(m_), z_mt(m_), ch_mt(m_), op=ALU.not_equal),
                   reads=[("chz", mt), ("zsc", mt)], writes=[("msk", mt)])
        for mt in range(4):
            mk.vop(lambda E, m_=mt:
                   E.tensor_tensor(ch_mt(m_), sig_mt(m_), z_mt(m_), op=ALU.mult),
                   reads=[("msk", mt), ("sig", mt)], writes=[("chp", mt)])
        for mt in range(4):
            mk.vop(lambda E, m_=mt:
                   E.reduce_sum(sc(f"Mden{m_}"), ch_mt(m_), axis=AX.X),
                   reads=[("chp", mt)], writes=[f"Mden{mt}"])
        for mt in range(4):
            mk.vop(lambda E, m_=mt:
                   E.reciprocal(sc(f"Mri{m_}"), sc(f"Mden{m_}")),
                   reads=[f"Mden{mt}"], writes=[f"Mri{mt}"])
        for mt in range(4):
            mk.vop(lambda E, m_=mt:
                   E.tensor_scalar(WDN[:, 16 * m_:16 * m_ + 16], ch_mt(m_),
                                   sc(f"Mri{m_}"), MOE_SCALE,
                                   op0=ALU.mult, op1=ALU.mult),
                   reads=[("chp", mt), f"Mri{mt}"], writes=[("wd", mt)])
        mk.tap(f"q{q}_wd", WDN, [128, 64])

        mk.vop(lambda E: E.memset(accTok, 0.0), writes=["accTok_all"])
        for e in range(EXPERTS):
            prev_evv = 0
            for mh in range(8):
                psl = {}
                for t2 in (1, 0):
                    s = wslot()
                    buf_guard(("ws", s), "sync")
                    lw = mk.dma("sync", "ld", ws[s][:, 0:512], H["rw1g"][e, mh, t2])
                    mk.wait("tensor", "ld", lw)
                    bk = (2 * mh + t2) % 4
                    mk.use_bank(bk)
                    for k in range(4):
                        mk.top(lambda E, s_=s, k_=k, b_=bk:
                               E.matmul(pb[b_][:, 0:512],
                                        ws[s_][:, 128 * k_:128 * k_ + 128],
                                        xlT[:, 512 * k_:512 * k_ + 512],
                                        start=(k_ == 0), stop=(k_ == 3)),
                               inc=(k == 3))
                    psl[t2] = (bk, mk.cnt["mm"])
                    set_guard(("ws", s), "mm", mk.cnt["mm"])
                bkg, mmg = psl[1]
                bkl, mml = psl[0]
                mk.wait("scalar", "mm", mmg)
                mk.wait("scalar", "ev", prev_evv)
                e1 = mk.sop(lambda E, b_=bkg: E.activation(LNB[:, 0:512], pb[b_],
                                                           AF.Relu))
                e2 = mk.sop(lambda E: E.activation(LNB[:, 512:1024], LNB[:, 0:512],
                                                   AF.Square))
                mk.bank_done(bkg, "es", e1)
                mk.wait("vector", "es", e2)
                mk.wait("vector", "mm", mml)
                evv = mk.vop(lambda E, m_=mh, b_=bkl:
                             E.tensor_tensor(hidET(m_), LNB[:, 512:1024], pb[b_],
                                             op=ALU.mult),
                             reads=["LNB2"], writes=[("hidET", mh)])
                mk.bank_done(bkl, "ev", evv)
                prev_evv = evv
            mk.wait("tensor", "ev", prev_evv)
            if e == 0:
                mk.tap(f"q{q}_hidE0", R2[:, 4096:8192], [128, 4096])
            buf_guard(("rw2b", 0), "sync")
            lw = mk.dma("sync", "ld", RW2B, H["rw2"][e])
            mk.wait("tensor", "ld", lw)
            for mt in range(4):
                mk.use_bank(4 + mt)
                for k in range(8):
                    mk.top(lambda E, k_=k, m_=mt:
                           E.matmul(pb[4 + m_][:, 0:512],
                                    hidET(k_)[:, 128 * m_:128 * m_ + 128],
                                    RW2B[:, 512 * k_:512 * k_ + 512],
                                    start=(k_ == 0), stop=(k_ == 7)),
                           inc=(k == 7))
            set_guard(("rw2b", 0), "mm", mk.cnt["mm"])
            mk.wait("vector", "mm", mk.cnt["mm"])
            for mt in range(4):
                evv = mk.vop(lambda E, m_=mt, e_=e:
                             E.scalar_tensor_tensor(
                                 accTok[:, 512 * m_:512 * m_ + 512],
                                 pb[4 + m_],
                                 WDN[:, 16 * m_ + e_:16 * m_ + e_ + 1],
                                 accTok[:, 512 * m_:512 * m_ + 512],
                                 op0=ALU.mult, op1=ALU.add),
                             reads=[("accTok", mt)], writes=[("accTok", mt)])
                mk.bank_done(4 + mt, "ev", evv)
            if e == 0:
                mk.tap(f"q{q}_accE0", accTok, [128, 2048])
        acc_ready = mk.cnt["ev"]
        mk.tap(f"q{q}_accTok", accTok, [128, 2048])

        mk.wait("tensor", "ev", acc_ready)
        for mt in range(4):
            bk = mt % 2
            mk.use_bank(bk)
            for j in range(4):
                mk.top(lambda E, m_=mt, j_=j, b_=bk:
                       E.transpose(pb[b_][0:128, 128 * j_:128 * j_ + 128],
                                   accTok[:, 512 * m_ + 128 * j_:
                                          512 * m_ + 128 * (j_ + 1)],
                                   IDT), inc=(j == 3))
            mmv = mk.cnt["mm"]
            mk.wait("vector", "mm", mmv)
            evv = None
            for j in range(4):
                evv = mk.vop(lambda E, m_=mt, j_=j, b_=bk:
                             E.tensor_copy(
                                 accT[:, 512 * j_ + 128 * m_:512 * j_ + 128 * m_ + 128],
                                 pb[b_][:, 128 * j_:128 * j_ + 128]),
                             writes=["accT"])
            mk.bank_done(bk, "ev", evv)
        accT_ready = mk.cnt["ev"]
        mk.wait("tensor", "ev", accT_ready)

        for mt in range(4):
            mk.vop(lambda E, m_=mt: E.memset(sc(f"F_s2{m_}"), 0.0),
                   writes=[f"F_s2{mt}"], inc=False)
            mk.vop(lambda E, m_=mt: E.memset(sc(f"F_q2{m_}"), 0.0),
                   writes=[f"F_q2{mt}"], inc=False)
        for n in range(8):
            s = wslot()
            buf_guard(("ws", s), "sync")
            lw = mk.dma("sync", "ld", ws[s], H["sw2"][n])
            s2 = wslot()
            buf_guard(("ws", s2), "sync")
            lw2 = mk.dma("sync", "ld", ws[s2][:, 0:1024], H["wup"][n])
            mk.wait("tensor", "ld", lw2)
            for mt in range(4):
                bk = mt % 4
                mk.use_bank(bk)
                for k in range(8):
                    mk.top(lambda E, s_=s, k_=k, m_=mt, b_=bk:
                           E.matmul(pb[b_][:, 0:256],
                                    hidT(k_)[:, 128 * m_:128 * m_ + 128],
                                    ws[s_][:, 256 * k_:256 * k_ + 256],
                                    start=(k_ == 0), stop=False), inc=False)
                for j in range(4):
                    mk.top(lambda E, s_=s2, j_=j, m_=mt, b_=bk:
                           E.matmul(pb[b_][:, 0:256],
                                    accT[:, 512 * j_ + 128 * m_:
                                         512 * j_ + 128 * m_ + 128],
                                    ws[s_][:, 256 * j_:256 * j_ + 256],
                                    start=False, stop=(j_ == 3)),
                           inc=(j == 3))
                mmv = mk.cnt["mm"]
                mk.wait("scalar", "mm", mmv)
                e1 = mk.sop(lambda E, m_=mt, n_=n, b_=bk:
                            E.activation(ffn_mt(m_)[:, 256 * n_:256 * n_ + 256],
                                         pb[b_][:, 0:256], AF.Identity,
                                         accum_out=sc("F_ps")))
                e2 = mk.sop(lambda E, b_=bk:
                            E.activation(LNB[:, 0:256], pb[b_][:, 0:256],
                                         AF.Square, accum_out=sc("F_pq")))
                mk.bank_done(bk, "es", e2)
                mk.wait("vector", "es", e2)
                mk.vop(lambda E, m_=mt:
                       E.tensor_add(sc(f"F_s2{m_}"), sc(f"F_s2{m_}"), sc("F_ps")),
                       reads=["F_ps", f"F_s2{mt}"], writes=[f"F_s2{mt}"])
                mk.vop(lambda E, m_=mt:
                       E.tensor_add(sc(f"F_q2{m_}"), sc(f"F_q2{m_}"), sc("F_pq")),
                       reads=["F_pq", f"F_q2{mt}"], writes=[f"F_q2{mt}"])
            set_guard(("ws", s), "mm", mk.cnt["mm"])
            set_guard(("ws", s2), "mm", mk.cnt["mm"])
        ln_math("F", range(4))
        for mt in range(4):
            mk.vop(lambda E, m_=mt:
                   E.tensor_add(sc(f"F_bc{m_}"), sc(f"D_mu{m_}"), sc(f"F_nmr{m_}")),
                   reads=[f"F_nmr{mt}"], writes=[f"F_bc{mt}"])
        mk.wait("scalar", "ev", mk.cnt["ev"])
        mk.tap(f"q{q}_ffn", R1[:, 0:8192], [128, 8192])

        outb = [OUTB[:, 0:2048], OUTB[:, 2048:4096]]
        for mt in range(4):
            cur = mt % 2
            buf_guard(("outb", cur), "scalar")
            for rnd in range(4):
                bk = 4 + (rnd % 2)
                mk.use_bank(bk)
                for j in range(4):
                    c = rnd * 4 + j
                    mk.top(lambda E, m_=mt, c_=c, j_=j, b_=bk:
                           E.transpose(pb[b_][0:128, 128 * j_:128 * (j_ + 1)],
                                       ln3T_blk(m_, c_), IDT), inc=(j == 3))
                mmv = mk.cnt["mm"]
                mk.wait("scalar", "mm", mmv)
                eh = mk.sop(lambda E, m_=mt, r_=rnd, c=cur, b_=bk:
                            E.activation(outb[c][:, 512 * r_:512 * (r_ + 1)],
                                         pb[b_][:, 0:512], AF.Identity,
                                         scale=sc(f"D_std{m_}"),
                                         bias=sc(f"F_bc{m_}")))
                mk.bank_done(bk, "es", eh)
            mk.wait("vector", "es", mk.cnt["es"])
            evo = mk.vop(lambda E, m_=mt, c=cur:
                         E.scalar_tensor_tensor(outb[c], ffn_mt(m_),
                                                sc(f"F_rstd{m_}"), outb[c],
                                                op0=ALU.mult, op1=ALU.add),
                         reads=[("ffn", mt), ("outb", cur)],
                         writes=[("outb", cur)])
            mk.wait("gpsimd", "ev", evo)
            r0 = q * QT + 128 * mt
            stv = mk.dma("gpsimd", "st", OUT[r0:r0 + 128, :], outb[cur])
            set_guard(("outb", cur), "st", stv)
        for i in range(5):
            set_guard(("xb", i), "es", mk.cnt["es"])

    mk.wait("gpsimd", "st", mk.cnt["st"])

    with (nc.semaphore("s_ld") as s_ld, nc.semaphore("s_lg") as s_lg,
          nc.semaphore("s_st") as s_st, nc.semaphore("s_mm") as s_mm,
          nc.semaphore("s_ev") as s_ev, nc.semaphore("s_es") as s_es,
          nc.Block() as block):
        mk.sems.update({"ld": s_ld, "lg": s_lg, "st": s_st,
                        "mm": s_mm, "ev": s_ev, "es": s_es})

        @block.sync
        def _(E):
            for fn in mk.ops["sync"]:
                fn(E)

        @block.gpsimd
        def _(E):
            for fn in mk.ops["gpsimd"]:
                fn(E)

        @block.tensor
        def _(E):
            for fn in mk.ops["tensor"]:
                fn(E)

        @block.vector
        def _(E):
            for fn in mk.ops["vector"]:
                fn(E)

        @block.scalar
        def _(E):
            for fn in mk.ops["scalar"]:
                fn(E)
    return nc, mk


def kernel(**inp):
    w = _prepack(inp)
    x = np.asarray(inp["L0_attn_x"], np.float32)
    nc, mk = build()
    in_maps = []
    for i in range(8):
        m = dict(w)
        m["x"] = np.ascontiguousarray(x[i])
        in_maps.append(m)
    res = run_bass_kernel_spmd(nc, in_maps, core_ids=list(range(8)))
    return np.stack([res.results[i]["out"] for i in range(8)], axis=0)


# version 03
# speedup vs baseline: 1.0179x; 1.0179x over previous
import numpy as np
import concourse.bass as bass
import concourse.mybir as mybir
from concourse.bass_utils import run_bass_kernel_spmd

F32 = mybir.dt.float32
AF = mybir.ActivationFunctionType
ALU = mybir.AluOpType
AX = mybir.AxisListType

D = 2048
NTOK = 2048
NQ = 4
QT = 512
QH = 528
NHP = 16
HD = 64
EPS = 1e-5
NEG = -1e30
EXPERTS = 16
MOE_SCALE = 2.5

ENGINES = ("sync", "gpsimd", "tensor", "vector", "scalar")


def _prepack(inp):
    f = np.float32
    qkv_w = np.asarray(inp["L0_attn_qkv_w"], f)
    proj_w = np.asarray(inp["L0_attn_proj_w"], f)
    proj_b = np.asarray(inp["L0_attn_proj_b"], f)
    sw1 = np.asarray(inp["L0_ffn_sw1"], f)
    swg = np.asarray(inp["L0_ffn_swg"], f)
    sw2 = np.asarray(inp["L0_ffn_sw2"], f)
    rw1 = np.asarray(inp["L0_ffn_rw1"], f)
    rwg = np.asarray(inp["L0_ffn_rwg"], f)
    rw2 = np.asarray(inp["L0_ffn_rw2"], f)
    Wg = np.asarray(inp["L0_ffn_Wg"], f)
    Wdown = np.asarray(inp["L0_ffn_Wdown"], f)
    Wup = np.asarray(inp["L0_ffn_Wup"], f)
    biases = np.asarray(inp["L0_ffn_biases"], f)

    w = {}
    t = qkv_w.reshape(3, 16, 128, 16, 128)
    w["qkvw"] = np.ascontiguousarray(t.transpose(1, 0, 4, 3, 2))
    t = proj_w.reshape(16, 128, 16, 128)
    w["pj"] = np.ascontiguousarray(t.transpose(0, 3, 2, 1))
    w["pjb"] = np.ascontiguousarray(proj_b.reshape(1, 2048))
    a = sw1.reshape(8, 128, 16, 128).transpose(0, 3, 2, 1)
    b = swg.reshape(8, 128, 16, 128).transpose(0, 3, 2, 1)
    w["sw1g"] = np.ascontiguousarray(np.stack([a, b], axis=1))
    w["wdn"] = np.ascontiguousarray(
        Wdown.reshape(4, 128, 16, 128).transpose(0, 3, 2, 1))
    w["wg"] = np.ascontiguousarray(Wg.reshape(16, 16, 128).transpose(2, 1, 0))
    a = rw1.reshape(16, 8, 128, 4, 128).transpose(0, 1, 4, 3, 2)
    b = rwg.reshape(16, 8, 128, 4, 128).transpose(0, 1, 4, 3, 2)
    w["rw1g"] = np.ascontiguousarray(np.stack([a, b], axis=2))
    w["rw2"] = np.ascontiguousarray(
        rw2.reshape(16, 512, 8, 128).transpose(0, 3, 2, 1))
    w["sw2"] = np.ascontiguousarray(
        sw2.reshape(8, 256, 8, 128).transpose(0, 3, 2, 1))
    w["wup"] = np.ascontiguousarray(
        Wup.reshape(8, 256, 4, 128).transpose(0, 3, 2, 1))
    w["bias_bc"] = np.ascontiguousarray(np.broadcast_to(biases, (128, 16)))

    p_i = np.arange(128)[:, None]
    c_i = np.arange(143)[None, :]
    o = c_i - p_i - 8
    band = (o >= -8) & (o < 8)
    mval = (8.0 - np.abs(o) / 2.0) / 8.0
    m2, bb = [], []
    for v in range(3):
        valid = band.copy()
        if v == 0:
            valid &= (c_i - 8 >= 0)
        if v == 2:
            valid &= (c_i - 8 < 128)
        m2.append(np.where(valid, mval, 1.0).astype(f))
        bb.append(np.where(valid, 0.0, NEG).astype(f))
    w["maskM"] = np.ascontiguousarray(np.stack(m2, axis=1))
    w["maskB"] = np.ascontiguousarray(np.stack(bb, axis=1))
    w["ident"] = np.eye(128, dtype=f)
    w["onesrow"] = np.ones((1, 128), dtype=f)
    w["consts"] = np.ascontiguousarray(
        np.broadcast_to(np.array([-1.0, EPS], f), (128, 2)))
    return w


class MK:
    def __init__(self, nc, taps=()):
        self.nc = nc
        self.taps = set(taps)
        self.dbg = {}
        self.ops = {e: [] for e in ENGINES}
        self.cnt = {s: 0 for s in ("ld", "lg", "st", "mm", "ev", "es")}
        self.waited = {}
        self.sems = {}
        self.vw = set()
        self.vr = set()
        self.bankg = [None] * 8

    def emit(self, eng, fn):
        self.ops[eng].append(fn)

    def wait(self, eng, sem, val):
        if not val or val <= 0:
            return
        k = (eng, sem)
        if self.waited.get(k, 0) >= val:
            return
        self.waited[k] = val
        self.emit(eng, lambda E, s=sem, v=val: E.wait_ge(self.sems[s], v))

    def dma(self, eng, sem, out_ap, in_ap):
        self.cnt[sem] += 16
        v = self.cnt[sem]
        self.emit(eng, lambda E, o=out_ap, i=in_ap, s=sem:
                  E.dma_start(out=o, in_=i).then_inc(self.sems[s], 16))
        return v

    def top(self, fn, inc=True):
        if inc:
            self.cnt["mm"] += 1
            self.emit("tensor", lambda E, f=fn: f(E).then_inc(self.sems["mm"], 1))
            return self.cnt["mm"]
        self.emit("tensor", fn)
        return None

    def vop(self, fn, reads=(), writes=(), inc=True):
        r, w = set(reads), set(writes)
        if (r & self.vw) or (w & self.vr) or (w & self.vw):
            self.emit("vector", lambda E: E.drain())
        self.vr, self.vw = r, w
        if inc:
            self.cnt["ev"] += 1
            self.emit("vector", lambda E, f=fn: f(E).then_inc(self.sems["ev"], 1))
            return self.cnt["ev"]
        self.emit("vector", fn)
        return None

    def sop(self, fn, inc=True):
        if inc:
            self.cnt["es"] += 1
            self.emit("scalar", lambda E, f=fn: f(E).then_inc(self.sems["es"], 1))
            return self.cnt["es"]
        self.emit("scalar", fn)
        return None

    def use_bank(self, b):
        g = self.bankg[b]
        if g:
            self.wait("tensor", g[0], g[1])

    def bank_done(self, b, sem, val):
        self.bankg[b] = (sem, val)

    def fence(self, eng):
        for s in ("mm", "ev", "es"):
            self.wait(eng, s, self.cnt[s])

    def tap(self, name, ap, shape):
        if name not in self.taps:
            return
        d = self.nc.dram_tensor("dbg_" + name, list(shape), F32,
                                kind="ExternalOutput").ap()
        self.dbg[name] = d
        for s in ("mm", "ev", "es"):
            self.wait("gpsimd", s, self.cnt[s])
        self.dma("gpsimd", "st", d, ap)


def build(taps=(), nq=NQ):
    nc = bass.Bass("TRN2", target_bir_lowering=False)
    mk = MK(nc, taps)
    f32 = F32

    X = nc.dram_tensor("x", [NTOK, D], f32, kind="ExternalInput").ap()
    OUT = nc.dram_tensor("out", [NTOK, D], f32, kind="ExternalOutput").ap()
    H = {}
    for n, s in [("qkvw", [16, 3, 128, 16, 128]), ("pj", [16, 128, 16, 128]),
                 ("pjb", [1, 2048]), ("sw1g", [8, 2, 128, 16, 128]),
                 ("wdn", [4, 128, 16, 128]), ("wg", [128, 16, 16]),
                 ("rw1g", [16, 8, 2, 128, 4, 128]), ("rw2", [16, 128, 8, 512]),
                 ("sw2", [8, 128, 8, 256]), ("wup", [8, 128, 4, 256]),
                 ("bias_bc", [128, 16]), ("maskM", [128, 3, 143]),
                 ("maskB", [128, 3, 143]), ("ident", [128, 128]),
                 ("onesrow", [1, 128]), ("consts", [128, 2])]:
        H[n] = nc.dram_tensor(n, s, f32, kind="ExternalInput").ap()

    R1 = nc.alloc_sbuf_tensor("R1", [128, 16 * QH], f32).ap()
    R2 = nc.alloc_sbuf_tensor("R2", [128, 16 * 512], f32).ap()
    R34 = nc.alloc_sbuf_tensor("R34", [128, 4 * 2048], f32).ap()
    R7 = nc.alloc_sbuf_tensor("R7", [128, 12288], f32).ap()
    LNB = nc.alloc_sbuf_tensor("LNB", [128, 2048], f32).ap()
    STT = nc.alloc_sbuf_tensor("STT", [128, 192], f32).ap()
    M8S = nc.alloc_sbuf_tensor("M8S", [128, 32], f32).ap()
    MM2 = nc.alloc_sbuf_tensor("MM2", [128, 3 * 143], f32).ap()
    MB2 = nc.alloc_sbuf_tensor("MB2", [128, 3 * 143], f32).ap()
    IDT = nc.alloc_sbuf_tensor("IDT", [128, 128], f32).ap()
    CST = nc.alloc_sbuf_tensor("CST", [128, 2], f32).ap()
    ONE = nc.alloc_sbuf_tensor("ONE", [1, 128], f32).ap()
    PJB = nc.alloc_sbuf_tensor("PJB", [1, 2048], f32).ap()
    BIA = nc.alloc_sbuf_tensor("BIA", [128, 16], f32).ap()
    WDN = nc.alloc_sbuf_tensor("WDS", [128, 64], f32).ap()
    OUTB = nc.alloc_sbuf_tensor("OUTB", [128, 2 * 2048], f32).ap()
    RW2B = nc.alloc_sbuf_tensor("RW2B", [128, 8 * 512], f32).ap()
    nc.const_aps.register(f32, -1.0, CST[:, 0:1])
    nc.const_aps.register(f32, EPS, CST[:, 1:2])

    pb = [nc.alloc_psum_tensor(f"pb{i}", [128, 512], f32).ap() for i in range(8)]

    def r7(a, b):
        return R7[:, a:b]

    def ln1T(c):
        return R1[:, c * QH:(c + 1) * QH]

    def ffn_mt(mt):
        return R1[:, mt * 2048:(mt + 1) * 2048]

    def aoT(hp):
        return R2[:, hp * 512:(hp + 1) * 512]

    def hidT(mh):
        return R2[:, mh * 512:(mh + 1) * 512]

    def hidET(mh):
        return R2[:, 4096 + mh * 512:4096 + (mh + 1) * 512]

    def projh(mt):
        return R34[:, mt * 2048:(mt + 1) * 2048]

    def ln3T_blk(mt, c):
        return R34[:, mt * 2048 + c * 128:mt * 2048 + (c + 1) * 128]



    st_col = {}

    def sc(name):
        if name not in st_col:
            st_col[name] = len(st_col)
            assert len(st_col) <= 192, name
        i = st_col[name]
        return STT[:, i:i + 1]

    ld0 = 0
    for dst, src in [(MM2, H["maskM"]), (MB2, H["maskB"]), (IDT, H["ident"]),
                     (CST, H["consts"]), (ONE, H["onesrow"]), (PJB, H["pjb"]),
                     (BIA, H["bias_bc"])]:
        ld0 = mk.dma("sync", "ld", dst, src)
    for e in ("vector", "scalar", "tensor"):
        mk.wait(e, "ld", ld0)

    guard = {}

    def buf_guard(key, eng):
        g = guard.get(key)
        if g:
            mk.wait(eng, g[0], g[1])

    def set_guard(key, sem, val):
        guard[key] = (sem, val)

    def ln_math(pre, cols, widths=None):
        widths = widths or [128] * len(cols)
        for m, w in zip(cols, widths):
            mk.vop(lambda E, m_=m, w_=w:
                   E.tensor_scalar_mul(sc(f"{pre}_mu{m_}")[0:w_],
                                       sc(f"{pre}_s2{m_}")[0:w_], 1.0 / D),
                   reads=[f"{pre}_s2{m}"], writes=[f"{pre}_mu{m}"])
        for m, w in zip(cols, widths):
            mk.vop(lambda E, m_=m, w_=w:
                   E.tensor_scalar_mul(sc(f"{pre}_msq{m_}")[0:w_],
                                       sc(f"{pre}_q2{m_}")[0:w_], 1.0 / D),
                   reads=[f"{pre}_q2{m}"], writes=[f"{pre}_msq{m}"])
        for m, w in zip(cols, widths):
            mk.vop(lambda E, m_=m, w_=w:
                   E.scalar_tensor_tensor(sc(f"{pre}_nv{m_}")[0:w_],
                                          sc(f"{pre}_mu{m_}")[0:w_],
                                          sc(f"{pre}_mu{m_}")[0:w_],
                                          sc(f"{pre}_msq{m_}")[0:w_],
                                          op0=ALU.mult, op1=ALU.subtract),
                   reads=[f"{pre}_mu{m}", f"{pre}_msq{m}"],
                   writes=[f"{pre}_nv{m}"])
        mk.wait("scalar", "ev", mk.cnt["ev"])
        for m, w in zip(cols, widths):
            mk.sop(lambda E, m_=m, w_=w:
                   E.activation(sc(f"{pre}_std{m_}")[0:w_],
                                sc(f"{pre}_nv{m_}")[0:w_],
                                AF.Sqrt, bias=EPS, scale=-1.0))
        mk.wait("vector", "es", mk.cnt["es"])
        for m, w in zip(cols, widths):
            mk.vop(lambda E, m_=m, w_=w:
                   E.reciprocal(sc(f"{pre}_rstd{m_}")[0:w_],
                                sc(f"{pre}_std{m_}")[0:w_]),
                   reads=[f"{pre}_std{m}"], writes=[f"{pre}_rstd{m}"])
        for m, w in zip(cols, widths):
            mk.vop(lambda E, m_=m, w_=w:
                   E.scalar_tensor_tensor(sc(f"{pre}_nmr{m_}")[0:w_],
                                          sc(f"{pre}_mu{m_}")[0:w_],
                                          -1.0, sc(f"{pre}_rstd{m_}")[0:w_],
                                          op0=ALU.mult, op1=ALU.mult),
                   reads=[f"{pre}_mu{m}", f"{pre}_rstd{m}"],
                   writes=[f"{pre}_nmr{m}"])
        return mk.cnt["ev"]

    for q in range(nq):
        g0 = q * QT - 8
        mk.fence("gpsimd")
        mk.wait("tensor", "ev", mk.cnt["ev"])

        xb = [r7(2048 * i, 2048 * (i + 1)) for i in range(5)]
        widths = [128, 128, 128, 128, 16]
        stA = []
        for t in range(5):
            w = widths[t]
            r0 = g0 + 128 * t
            cur = t
            g = guard.get(("xb", cur))
            if g:
                mk.wait("gpsimd", g[0], g[1])
            lo, hi = max(r0, 0), min(r0 + w, NTOK)
            p0 = lo - r0
            if p0 > 0 or hi - r0 < w:
                if g:
                    mk.wait("vector", g[0], g[1])
                mvz = mk.vop(lambda E, b=xb[cur], a=p0, ww=w:
                             (E.memset(b[0:a, :], 0.0) if a > 0 else
                              E.memset(b[0:ww, :], 0.0)),
                             writes=[("xb", cur)])
                mk.wait("gpsimd", "ev", mvz)
            lg = mk.dma("gpsimd", "lg", xb[cur][p0:p0 + (hi - lo), :], X[lo:hi, :])
            mk.wait("scalar", "lg", lg)
            mk.wait("vector", "lg", lg)
            es1 = mk.sop(lambda E, b=xb[cur], w_=w, t_=t:
                         E.activation(LNB[0:w_, :], b[0:w_, :], AF.Square,
                                      accum_out=sc(f"A_q2{t_}")[0:w_]))
            mk.vop(lambda E, b=xb[cur], w_=w, t_=t:
                   E.reduce_sum(sc(f"A_s2{t_}")[0:w_], b[0:w_, :], axis=AX.X),
                   reads=[("xb", cur)], writes=[f"A_s2{t}"])
            stA.append((t, w, cur, es1))
        mk.wait("vector", "es", mk.cnt["es"])
        ln_math("A", [t for t, _, _, _ in stA], [w for _, w, _, _ in stA])
        mk.wait("scalar", "ev", mk.cnt["ev"])
        mm_prev_t = 0
        for t, w, cur, _ in stA:
            mk.wait("scalar", "mm", mm_prev_t)
            esap = mk.sop(lambda E, b=xb[cur], w_=w, t_=t:
                          E.activation(LNB[0:w_, :], b[0:w_, :], AF.Identity,
                                       scale=sc(f"A_rstd{t_}")[0:w_],
                                       bias=sc(f"A_nmr{t_}")[0:w_]))
            set_guard(("xb", cur), "es", esap)
            mk.wait("tensor", "es", esap)
            for rnd in range(4):
                bk = 6 + (rnd % 2)
                mk.use_bank(bk)
                for j in range(4):
                    c = rnd * 4 + j
                    mk.top(lambda E, w_=w, c_=c, j_=j, b_=bk:
                           E.transpose(pb[b_][0:128, j_ * 128:j_ * 128 + w_],
                                       LNB[0:w_, c_ * 128:(c_ + 1) * 128],
                                       IDT[0:w_, 0:w_]),
                           inc=(j == 3))
                mmv = mk.cnt["mm"]
                mk.wait("vector", "mm", mmv)
                evv = None
                for j in range(4):
                    c = rnd * 4 + j
                    evv = mk.vop(lambda E, c_=c, j_=j, b_=bk, w_=w, t_=t:
                                 E.tensor_copy(
                                     R1[:, c_ * QH + 128 * t_:c_ * QH + 128 * t_ + w_],
                                     pb[b_][:, 128 * j_:128 * j_ + w_]),
                                 writes=["ln1T"])
                mk.bank_done(bk, "ev", evv)
            mm_prev_t = mk.cnt["mm"]
        ln1_ready = mk.cnt["ev"]
        mk.tap(f"q{q}_ln1T", R1[:, 0:16 * QH], [128, 16 * QH])

        mk.fence("sync")
        wq = [r7(0, 2048), r7(2048, 4096)]
        qkvT = r7(4096, 4096 + 3 * QH)

        def qv(part):
            return qkvT[:, part * QH:(part + 1) * QH]
        vtok = r7(5680, 6000)
        sct = r7(6000, 6143)
        pex = r7(6144, 6287)
        atT = r7(6288, 6544)
        mk.wait("tensor", "ev", ln1_ready)

        for hp in range(NHP):
            evq = [0, 0, 0]
            for part in range(3):
                cur = (3 * hp + part) % 2
                buf_guard(("wq", cur), "sync")
                lw = mk.dma("sync", "ld", wq[cur], H["qkvw"][hp, part])
                mk.wait("tensor", "ld", lw)
                for sl in range(2):
                    bk = (2 * part + sl) % 4
                    mk.use_bank(bk)
                    for k in range(16):
                        mk.top(lambda E, c=cur, k_=k, s=sl, b_=bk:
                               E.matmul(pb[b_][:, 0:264],
                                        wq[c][:, k_ * 128:(k_ + 1) * 128],
                                        ln1T(k_)[:, s * 264:s * 264 + 264],
                                        start=(k_ == 0), stop=(k_ == 15)),
                               inc=(k == 15))
                    mmv = mk.cnt["mm"]
                    mk.wait("vector", "mm", mmv)
                    if part == 0:
                        evv = mk.vop(lambda E, s=sl, b_=bk:
                                     E.tensor_scalar_mul(
                                         qkvT[:, s * 264:s * 264 + 264],
                                         pb[b_][:, 0:264], HD ** -0.5),
                                     writes=["qkvT"])
                    else:
                        evv = mk.vop(lambda E, p_=part, s=sl, b_=bk:
                                     E.tensor_copy(qv(p_)[:, s * 264:s * 264 + 264],
                                                   pb[b_][:, 0:264]),
                                     writes=["qkvT"])
                    mk.bank_done(bk, "ev", evv)
                set_guard(("wq", cur), "mm", mk.cnt["mm"])
                evq[part] = mk.cnt["ev"]
            if hp == 0:
                mk.tap(f"q{q}_qkvT", qkvT, [128, 3 * QH])

            for h01 in range(2):
                hb = 64 * h01
                mk.wait("tensor", "ev", evq[2])
                mk.use_bank(4)
                for v5 in range(5):
                    wv = 128 if v5 < 4 else 16
                    mk.top(lambda E, v_=v5, w_=wv, h_=hb:
                           E.transpose(pb[4][0:w_, v_ * 64:v_ * 64 + 64],
                                       qv(2)[h_:h_ + 64, v_ * 128:v_ * 128 + w_],
                                       IDT[h_:h_ + 64, h_:h_ + 64]),
                           inc=(v5 == 4))
                mmv = mk.cnt["mm"]
                mk.wait("vector", "mm", mmv)
                ev_vt = mk.vop(lambda E: E.tensor_copy(vtok, pb[4][:, 0:320]),
                               writes=["vtok"])
                mk.bank_done(4, "ev", ev_vt)

                def emit_qk(b):
                    gq = 4 * q + b
                    var = 0 if gq == 0 else (2 if gq == 15 else 1)
                    bk = 4 + (b % 2)
                    mk.use_bank(bk)
                    mk.top(lambda E, b_=b, h_=hb, k_=bk:
                           E.matmul(pb[k_][:, 0:143],
                                    qv(0)[h_:h_ + 64, 8 + 128 * b_:136 + 128 * b_],
                                    qv(1)[h_:h_ + 64, 128 * b_:128 * b_ + 143],
                                    start=True, stop=False), inc=False)
                    mmv = mk.top(lambda E, v_=var, k_=bk:
                                 E.matmul(pb[k_][:, 0:143], IDT,
                                          MB2[:, v_ * 143:(v_ + 1) * 143],
                                          start=False, stop=True))
                    return b, var, bk, mmv

                def emit_tail(u):
                    b, var, bk, mmqk = u
                    mk.wait("vector", "mm", mmqk)
                    mk.vop(lambda E, k_=bk:
                           E.reduce_max(sc("B_nrm"), pb[k_][:, 0:143],
                                        axis=AX.X, negate=True),
                           writes=["B_nrm"])
                    ev_sc = mk.vop(lambda E, v_=var, k_=bk:
                                   E.tensor_tensor(sct, pb[k_][:, 0:143],
                                                   MM2[:, v_ * 143:(v_ + 1) * 143],
                                                   op=ALU.mult),
                                   writes=["sct"])
                    mk.bank_done(bk, "ev", ev_sc)
                    mk.wait("scalar", "ev", ev_sc)
                    es_e = mk.sop(lambda E:
                                  E.activation(pex, sct, AF.Exp, bias=sc("B_nrm"),
                                               accum_out=sc("B_rsum")))
                    mk.wait("vector", "es", es_e)
                    ev_ri = mk.vop(lambda E:
                                   E.reciprocal(sc("B_rinv"), sc("B_rsum")),
                                   reads=["B_rsum"], writes=["B_rinv"])
                    mk.wait("scalar", "ev", ev_ri)
                    es_pn = mk.sop(lambda E: E.mul(sct, pex, sc("B_rinv")))
                    mk.wait("tensor", "es", es_pn)
                    mk.use_bank(6)
                    mk.top(lambda E: E.transpose(pb[6][0:128, 0:128],
                                                 sct[:, 0:128], IDT), inc=False)
                    mmT = mk.top(lambda E: E.transpose(pb[6][0:15, 128:256],
                                                       sct[:, 128:143], IDT))
                    mk.wait("vector", "mm", mmT)
                    ev_aT = mk.vop(lambda E: E.tensor_copy(atT, pb[6][:, 0:256]),
                                   writes=["atT"])
                    mk.bank_done(6, "ev", ev_aT)
                    mk.wait("tensor", "ev", ev_aT)
                    mk.use_bank(7)
                    mk.top(lambda E, b_=b:
                           E.matmul(pb[7][0:64, 0:128],
                                    vtok[:, 64 * b_:64 * b_ + 64],
                                    atT[:, 0:128], start=True, stop=False),
                           inc=False)
                    mmAV = mk.top(lambda E, b_=b:
                                  E.matmul(pb[7][0:64, 0:128],
                                           vtok[0:15, 64 * b_ + 64:64 * b_ + 128],
                                           atT[0:15, 128:256],
                                           start=False, stop=True))
                    mk.wait("vector", "mm", mmAV)
                    ev_o = mk.vop(lambda E, b_=b, h_=hb, hp_=hp:
                                  E.tensor_copy(
                                      aoT(hp_)[h_:h_ + 64, 128 * b_:128 * b_ + 128],
                                      pb[7][0:64, 0:128]),
                                  writes=["aoT"])
                    mk.bank_done(7, "ev", ev_o)

                prev = None
                for b in range(4):
                    u = emit_qk(b)
                    if prev is not None:
                        emit_tail(prev)
                    prev = u
                emit_tail(prev)
        ao_ready = mk.cnt["ev"]
        mk.tap(f"q{q}_aoT", R2[:, 0:8192], [128, 8192])

        mk.fence("sync")
        mk.fence("gpsimd")
        wpj = [r7(0, 2048), r7(2048, 4096)]
        xr = [r7(4096, 6144), r7(6144, 8192)]
        mk.wait("tensor", "ev", ao_ready)
        for m in range(4):
            mk.vop(lambda E, m_=m: E.memset(sc(f"C_s2{m_}"), 0.0),
                   writes=[f"C_s2{m}"], inc=False)
            mk.vop(lambda E, m_=m: E.memset(sc(f"C_q2{m_}"), 0.0),
                   writes=[f"C_q2{m}"], inc=False)
        for n16 in range(16):
            cur = n16 % 2
            buf_guard(("wpj", cur), "sync")
            lw = mk.dma("sync", "ld", wpj[cur], H["pj"][n16])
            mk.wait("tensor", "ld", lw)
            for m in range(4):
                bk = m % 4
                mk.use_bank(bk)
                for k in range(16):
                    mk.top(lambda E, c=cur, k_=k, m_=m, b_=bk:
                           E.matmul(pb[b_][:, 0:128],
                                    aoT(k_)[:, 128 * m_:128 * m_ + 128],
                                    wpj[c][:, 128 * k_:128 * k_ + 128],
                                    start=(k_ == 0), stop=False), inc=False)
                mmv = mk.top(lambda E, n_=n16, b_=bk:
                             E.matmul(pb[b_][:, 0:128], ONE,
                                      PJB[0:1, 128 * n_:128 * n_ + 128],
                                      start=False, stop=True))
                mk.wait("scalar", "mm", mmv)
                e1 = mk.sop(lambda E, m_=m, n_=n16, b_=bk:
                            E.activation(projh(m_)[:, 128 * n_:128 * n_ + 128],
                                         pb[b_][:, 0:128], AF.Identity,
                                         accum_out=sc("C_ps")))
                e2 = mk.sop(lambda E, b_=bk:
                            E.activation(LNB[:, 0:128], pb[b_][:, 0:128],
                                         AF.Square, accum_out=sc("C_pq")))
                mk.bank_done(bk, "es", e2)
                mk.wait("vector", "es", e2)
                mk.vop(lambda E, m_=m:
                       E.tensor_add(sc(f"C_s2{m_}"), sc(f"C_s2{m_}"), sc("C_ps")),
                       reads=["C_ps", f"C_s2{m}"], writes=[f"C_s2{m}"])
                mk.vop(lambda E, m_=m:
                       E.tensor_add(sc(f"C_q2{m_}"), sc(f"C_q2{m_}"), sc("C_pq")),
                       reads=["C_pq", f"C_q2{m}"], writes=[f"C_q2{m}"])
            set_guard(("wpj", cur), "mm", mk.cnt["mm"])
        mk.tap(f"q{q}_proj", R34[:, 0:8192], [128, 8192])

        ln_math("C", range(4))
        mk.wait("scalar", "ev", mk.cnt["ev"])
        for m in range(4):
            cur = m % 2
            buf_guard(("xr", cur), "gpsimd")
            r0 = q * QT + 128 * m
            lg = mk.dma("gpsimd", "lg", xr[cur], X[r0:r0 + 128, :])
            mk.wait("scalar", "lg", lg)
            mk.wait("scalar", "ev", mk.cnt["ev"])
            e1 = mk.sop(lambda E, c=cur, m_=m:
                        E.activation(LNB, xr[c], AF.Identity, bias=sc(f"C_nmr{m_}")))
            set_guard(("xr", cur), "es", e1)
            mk.wait("vector", "es", e1)
            evh = mk.vop(lambda E, m_=m:
                         E.scalar_tensor_tensor(projh(m_), projh(m_),
                                                sc(f"C_rstd{m_}"), LNB,
                                                op0=ALU.mult, op1=ALU.add),
                         reads=["LNB", ("projh", m)], writes=[("projh", m)])
            mk.wait("scalar", "ev", evh)
            mk.sop(lambda E, m_=m:
                   E.activation(LNB, projh(m_), AF.Square,
                                accum_out=sc(f"D_q2{m_}")))
            mk.vop(lambda E, m_=m:
                   E.reduce_sum(sc(f"D_s2{m_}"), projh(m_), axis=AX.X),
                   reads=[("projh", m)], writes=[f"D_s2{m}"])
        mk.wait("vector", "es", mk.cnt["es"])
        ln_math("D", range(4))
        mk.wait("scalar", "ev", mk.cnt["ev"])
        mm_prev_t = 0
        for m in range(4):
            mk.wait("scalar", "mm", mm_prev_t)
            e1 = mk.sop(lambda E, m_=m:
                        E.activation(LNB, projh(m_), AF.Identity,
                                     scale=sc(f"D_rstd{m_}"), bias=sc(f"D_nmr{m_}")))
            mk.wait("tensor", "es", e1)
            for rnd in range(4):
                bk = 6 + (rnd % 2)
                mk.use_bank(bk)
                for j in range(4):
                    c = rnd * 4 + j
                    mk.top(lambda E, c_=c, j_=j, b_=bk:
                           E.transpose(pb[b_][0:128, j_ * 128:(j_ + 1) * 128],
                                       LNB[:, c_ * 128:(c_ + 1) * 128], IDT),
                           inc=(j == 3))
                mmv = mk.cnt["mm"]
                mk.wait("vector", "mm", mmv)
                evv = mk.vop(lambda E, m_=m, b_=bk, r_=rnd:
                             E.tensor_copy(
                                 R34[:, m_ * 2048 + r_ * 512:m_ * 2048 + (r_ + 1) * 512],
                                 pb[b_][:, 0:512]),
                             writes=["ln3T"])
                mk.bank_done(bk, "ev", evv)
            mm_prev_t = mk.cnt["mm"]
        ln3_ready = mk.cnt["ev"]
        mk.tap(f"q{q}_ln3T", R34[:, 0:8192], [128, 8192])

        mk.fence("sync")
        ws = [r7(0, 2048), r7(2048, 4096)]
        xlT = r7(4096, 6144)
        accTok = r7(6144, 8192)
        accT = r7(8192, 10240)
        wgb = r7(10240, 10496)
        prT = r7(10496, 11008)
        moe = r7(11008, 11264)
        mk.wait("tensor", "ev", ln3_ready)
        wcur = [0]

        def wslot():
            s = wcur[0] % 2
            wcur[0] += 1
            return s

        prev_evv = 0
        for mh in range(8):
            psl = {}
            for t2 in (1, 0):
                s = wslot()
                buf_guard(("ws", s), "sync")
                lw = mk.dma("sync", "ld", ws[s], H["sw1g"][mh, t2])
                mk.wait("tensor", "ld", lw)
                bk = (2 * mh + t2) % 4
                mk.use_bank(bk)
                for k in range(16):
                    for mt4 in range(4):
                        mk.top(lambda E, s_=s, k_=k, b_=bk, m_=mt4:
                               E.matmul(pb[b_][:, 128 * m_:128 * m_ + 128],
                                        ws[s_][:, 128 * k_:128 * k_ + 128],
                                        ln3T_blk(m_, k_),
                                        start=(k_ == 0 and m_ == 0),
                                        stop=(k_ == 15 and m_ == 3)),
                               inc=(k == 15 and mt4 == 3))
                psl[t2] = (bk, mk.cnt["mm"])
                set_guard(("ws", s), "mm", mk.cnt["mm"])
            bkg, mmg = psl[1]
            bkl, mml = psl[0]
            mk.wait("scalar", "mm", mmg)
            mk.wait("scalar", "ev", prev_evv)
            e1 = mk.sop(lambda E, b_=bkg: E.activation(LNB[:, 0:512], pb[b_], AF.Relu))
            e2 = mk.sop(lambda E: E.activation(LNB[:, 512:1024], LNB[:, 0:512],
                                               AF.Square))
            mk.bank_done(bkg, "es", e1)
            mk.wait("vector", "es", e2)
            mk.wait("vector", "mm", mml)
            evv = mk.vop(lambda E, m_=mh, b_=bkl:
                         E.tensor_tensor(hidT(m_), LNB[:, 512:1024], pb[b_],
                                         op=ALU.mult),
                         reads=["LNB2"], writes=[("hidT", mh)])
            mk.bank_done(bkl, "ev", evv)
            prev_evv = evv
        for ml in range(4):
            s = wslot()
            buf_guard(("ws", s), "sync")
            lw = mk.dma("sync", "ld", ws[s], H["wdn"][ml])
            mk.wait("tensor", "ld", lw)
            bk = ml % 4
            mk.use_bank(bk)
            for k in range(16):
                for mt4 in range(4):
                    mk.top(lambda E, s_=s, k_=k, b_=bk, m_=mt4:
                           E.matmul(pb[b_][:, 128 * m_:128 * m_ + 128],
                                    ws[s_][:, 128 * k_:128 * k_ + 128],
                                    ln3T_blk(m_, k_),
                                    start=(k_ == 0 and m_ == 0),
                                    stop=(k_ == 15 and m_ == 3)),
                           inc=(k == 15 and mt4 == 3))
            mmv = mk.cnt["mm"]
            set_guard(("ws", s), "mm", mmv)
            mk.wait("vector", "mm", mmv)
            evv = mk.vop(lambda E, m_=ml, b_=bk:
                         E.tensor_copy(xlT[:, 512 * m_:512 * m_ + 512], pb[b_]),
                         writes=["xlT"])
            mk.bank_done(bk, "ev", evv)
        mk.tap(f"q{q}_xlT", xlT, [128, 2048])
        lw = mk.dma("sync", "ld", wgb, H["wg"])
        mk.wait("tensor", "ld", lw)
        mk.use_bank(4)
        for k in range(16):
            for mt4 in range(4):
                mk.top(lambda E, k_=k, m_=mt4:
                       E.matmul(pb[4][0:16, 128 * m_:128 * m_ + 128],
                                wgb[:, 16 * k_:16 * k_ + 16],
                                ln3T_blk(m_, k_),
                                start=(k_ == 0 and m_ == 0),
                                stop=(k_ == 15 and m_ == 3)),
                       inc=(k == 15 and mt4 == 3))
        mmv = mk.cnt["mm"]
        mk.wait("vector", "mm", mmv)
        ev_pr = mk.vop(lambda E: E.tensor_copy(prT[0:16, :], pb[4][0:16, 0:512]),
                       writes=["prT"])
        mk.bank_done(4, "ev", ev_pr)
        mk.wait("tensor", "ev", ev_pr)
        mk.use_bank(5)
        for mt in range(4):
            mk.top(lambda E, m_=mt:
                   E.transpose(pb[5][0:128, 16 * m_:16 * m_ + 16],
                               prT[0:16, 128 * m_:128 * m_ + 128],
                               IDT[0:16, 0:16]),
                   inc=(mt == 3))
        mmv = mk.cnt["mm"]
        mk.wait("scalar", "mm", mmv)
        es_sig = mk.sop(lambda E: E.activation(moe[:, 0:64], pb[5][:, 0:64],
                                               AF.Sigmoid))
        mk.bank_done(5, "es", es_sig)
        mk.wait("vector", "es", es_sig)

        def sig_mt(mt):
            return moe[:, 16 * mt:16 * mt + 16]

        def z_mt(mt):
            return moe[:, 64 + 16 * mt:80 + 16 * mt]

        def ch_mt(mt):
            return moe[:, 128 + 16 * mt:144 + 16 * mt]

        for mt in range(4):
            mk.vop(lambda E, m_=mt: E.tensor_add(z_mt(m_), sig_mt(m_), BIA),
                   reads=[("sig", mt)], writes=[("zsc", mt)])
        for mt in range(4):
            mk.vop(lambda E, m_=mt: E.max(M8S[:, 8 * m_:8 * m_ + 8], z_mt(m_)),
                   reads=[("zsc", mt)], writes=[("m8", mt)])
        for mt in range(4):
            mk.vop(lambda E, m_=mt:
                   E.match_replace(ch_mt(m_), in_to_replace=M8S[:, 8 * m_:8 * m_ + 8],
                                   in_values=z_mt(m_), imm_value=NEG),
                   reads=[("m8", mt), ("zsc", mt)], writes=[("chz", mt)])
        for mt in range(4):
            mk.vop(lambda E, m_=mt:
                   E.tensor_tensor(z_mt(m_), z_mt(m_), ch_mt(m_), op=ALU.not_equal),
                   reads=[("chz", mt), ("zsc", mt)], writes=[("msk", mt)])
        for mt in range(4):
            mk.vop(lambda E, m_=mt:
                   E.tensor_tensor(ch_mt(m_), sig_mt(m_), z_mt(m_), op=ALU.mult),
                   reads=[("msk", mt), ("sig", mt)], writes=[("chp", mt)])
        for mt in range(4):
            mk.vop(lambda E, m_=mt:
                   E.reduce_sum(sc(f"Mden{m_}"), ch_mt(m_), axis=AX.X),
                   reads=[("chp", mt)], writes=[f"Mden{mt}"])
        for mt in range(4):
            mk.vop(lambda E, m_=mt:
                   E.reciprocal(sc(f"Mri{m_}"), sc(f"Mden{m_}")),
                   reads=[f"Mden{mt}"], writes=[f"Mri{mt}"])
        for mt in range(4):
            mk.vop(lambda E, m_=mt:
                   E.tensor_scalar(WDN[:, 16 * m_:16 * m_ + 16], ch_mt(m_),
                                   sc(f"Mri{m_}"), MOE_SCALE,
                                   op0=ALU.mult, op1=ALU.mult),
                   reads=[("chp", mt), f"Mri{mt}"], writes=[("wd", mt)])
        mk.tap(f"q{q}_wd", WDN, [128, 64])

        mk.vop(lambda E: E.memset(accTok, 0.0), writes=["accTok_all"])
        for e in range(EXPERTS):
            prev_evv = 0
            for mh in range(8):
                psl = {}
                for t2 in (1, 0):
                    s = wslot()
                    buf_guard(("ws", s), "sync")
                    lw = mk.dma("sync", "ld", ws[s][:, 0:512], H["rw1g"][e, mh, t2])
                    mk.wait("tensor", "ld", lw)
                    bk = (2 * mh + t2) % 4
                    mk.use_bank(bk)
                    for k in range(4):
                        mk.top(lambda E, s_=s, k_=k, b_=bk:
                               E.matmul(pb[b_][:, 0:512],
                                        ws[s_][:, 128 * k_:128 * k_ + 128],
                                        xlT[:, 512 * k_:512 * k_ + 512],
                                        start=(k_ == 0), stop=(k_ == 3)),
                               inc=(k == 3))
                    psl[t2] = (bk, mk.cnt["mm"])
                    set_guard(("ws", s), "mm", mk.cnt["mm"])
                bkg, mmg = psl[1]
                bkl, mml = psl[0]
                mk.wait("scalar", "mm", mmg)
                mk.wait("scalar", "ev", prev_evv)
                e1 = mk.sop(lambda E, b_=bkg: E.activation(LNB[:, 0:512], pb[b_],
                                                           AF.Relu))
                e2 = mk.sop(lambda E: E.activation(LNB[:, 512:1024], LNB[:, 0:512],
                                                   AF.Square))
                mk.bank_done(bkg, "es", e1)
                mk.wait("vector", "es", e2)
                mk.wait("vector", "mm", mml)
                evv = mk.vop(lambda E, m_=mh, b_=bkl:
                             E.tensor_tensor(hidET(m_), LNB[:, 512:1024], pb[b_],
                                             op=ALU.mult),
                             reads=["LNB2"], writes=[("hidET", mh)])
                mk.bank_done(bkl, "ev", evv)
                prev_evv = evv
            mk.wait("tensor", "ev", prev_evv)
            if e == 0:
                mk.tap(f"q{q}_hidE0", R2[:, 4096:8192], [128, 4096])
            buf_guard(("rw2b", 0), "sync")
            lw = mk.dma("sync", "ld", RW2B, H["rw2"][e])
            mk.wait("tensor", "ld", lw)
            for mt in range(4):
                mk.use_bank(4 + mt)
                for k in range(8):
                    mk.top(lambda E, k_=k, m_=mt:
                           E.matmul(pb[4 + m_][:, 0:512],
                                    hidET(k_)[:, 128 * m_:128 * m_ + 128],
                                    RW2B[:, 512 * k_:512 * k_ + 512],
                                    start=(k_ == 0), stop=(k_ == 7)),
                           inc=(k == 7))
            set_guard(("rw2b", 0), "mm", mk.cnt["mm"])
            mk.wait("vector", "mm", mk.cnt["mm"])
            for mt in range(4):
                evv = mk.vop(lambda E, m_=mt, e_=e:
                             E.scalar_tensor_tensor(
                                 accTok[:, 512 * m_:512 * m_ + 512],
                                 pb[4 + m_],
                                 WDN[:, 16 * m_ + e_:16 * m_ + e_ + 1],
                                 accTok[:, 512 * m_:512 * m_ + 512],
                                 op0=ALU.mult, op1=ALU.add),
                             reads=[("accTok", mt)], writes=[("accTok", mt)])
                mk.bank_done(4 + mt, "ev", evv)
            if e == 0:
                mk.tap(f"q{q}_accE0", accTok, [128, 2048])
        acc_ready = mk.cnt["ev"]
        mk.tap(f"q{q}_accTok", accTok, [128, 2048])

        mk.wait("tensor", "ev", acc_ready)
        for mt in range(4):
            bk = mt % 2
            mk.use_bank(bk)
            for j in range(4):
                mk.top(lambda E, m_=mt, j_=j, b_=bk:
                       E.transpose(pb[b_][0:128, 128 * j_:128 * j_ + 128],
                                   accTok[:, 512 * m_ + 128 * j_:
                                          512 * m_ + 128 * (j_ + 1)],
                                   IDT), inc=(j == 3))
            mmv = mk.cnt["mm"]
            mk.wait("vector", "mm", mmv)
            evv = None
            for j in range(4):
                evv = mk.vop(lambda E, m_=mt, j_=j, b_=bk:
                             E.tensor_copy(
                                 accT[:, 512 * j_ + 128 * m_:512 * j_ + 128 * m_ + 128],
                                 pb[b_][:, 128 * j_:128 * j_ + 128]),
                             writes=["accT"])
            mk.bank_done(bk, "ev", evv)
        accT_ready = mk.cnt["ev"]
        mk.wait("tensor", "ev", accT_ready)

        for mt in range(4):
            mk.vop(lambda E, m_=mt: E.memset(sc(f"F_s2{m_}"), 0.0),
                   writes=[f"F_s2{mt}"], inc=False)
            mk.vop(lambda E, m_=mt: E.memset(sc(f"F_q2{m_}"), 0.0),
                   writes=[f"F_q2{mt}"], inc=False)
        for n in range(8):
            s = wslot()
            buf_guard(("ws", s), "sync")
            lw = mk.dma("sync", "ld", ws[s], H["sw2"][n])
            s2 = wslot()
            buf_guard(("ws", s2), "sync")
            lw2 = mk.dma("sync", "ld", ws[s2][:, 0:1024], H["wup"][n])
            mk.wait("tensor", "ld", lw2)
            for mt in range(4):
                bk = mt % 4
                mk.use_bank(bk)
                for k in range(8):
                    mk.top(lambda E, s_=s, k_=k, m_=mt, b_=bk:
                           E.matmul(pb[b_][:, 0:256],
                                    hidT(k_)[:, 128 * m_:128 * m_ + 128],
                                    ws[s_][:, 256 * k_:256 * k_ + 256],
                                    start=(k_ == 0), stop=False), inc=False)
                for j in range(4):
                    mk.top(lambda E, s_=s2, j_=j, m_=mt, b_=bk:
                           E.matmul(pb[b_][:, 0:256],
                                    accT[:, 512 * j_ + 128 * m_:
                                         512 * j_ + 128 * m_ + 128],
                                    ws[s_][:, 256 * j_:256 * j_ + 256],
                                    start=False, stop=(j_ == 3)),
                           inc=(j == 3))
                mmv = mk.cnt["mm"]
                mk.wait("scalar", "mm", mmv)
                e1 = mk.sop(lambda E, m_=mt, n_=n, b_=bk:
                            E.activation(ffn_mt(m_)[:, 256 * n_:256 * n_ + 256],
                                         pb[b_][:, 0:256], AF.Identity,
                                         accum_out=sc("F_ps")))
                e2 = mk.sop(lambda E, b_=bk:
                            E.activation(LNB[:, 0:256], pb[b_][:, 0:256],
                                         AF.Square, accum_out=sc("F_pq")))
                mk.bank_done(bk, "es", e2)
                mk.wait("vector", "es", e2)
                mk.vop(lambda E, m_=mt:
                       E.tensor_add(sc(f"F_s2{m_}"), sc(f"F_s2{m_}"), sc("F_ps")),
                       reads=["F_ps", f"F_s2{mt}"], writes=[f"F_s2{mt}"])
                mk.vop(lambda E, m_=mt:
                       E.tensor_add(sc(f"F_q2{m_}"), sc(f"F_q2{m_}"), sc("F_pq")),
                       reads=["F_pq", f"F_q2{mt}"], writes=[f"F_q2{mt}"])
            set_guard(("ws", s), "mm", mk.cnt["mm"])
            set_guard(("ws", s2), "mm", mk.cnt["mm"])
        ln_math("F", range(4))
        for mt in range(4):
            mk.vop(lambda E, m_=mt:
                   E.tensor_add(sc(f"F_bc{m_}"), sc(f"D_mu{m_}"), sc(f"F_nmr{m_}")),
                   reads=[f"F_nmr{mt}"], writes=[f"F_bc{mt}"])
        mk.wait("scalar", "ev", mk.cnt["ev"])
        mk.tap(f"q{q}_ffn", R1[:, 0:8192], [128, 8192])

        outb = [OUTB[:, 0:2048], OUTB[:, 2048:4096]]
        for mt in range(4):
            cur = mt % 2
            buf_guard(("outb", cur), "scalar")
            for rnd in range(4):
                bk = 4 + (rnd % 2)
                mk.use_bank(bk)
                for j in range(4):
                    c = rnd * 4 + j
                    mk.top(lambda E, m_=mt, c_=c, j_=j, b_=bk:
                           E.transpose(pb[b_][0:128, 128 * j_:128 * (j_ + 1)],
                                       ln3T_blk(m_, c_), IDT), inc=(j == 3))
                mmv = mk.cnt["mm"]
                mk.wait("scalar", "mm", mmv)
                eh = mk.sop(lambda E, m_=mt, r_=rnd, c=cur, b_=bk:
                            E.activation(outb[c][:, 512 * r_:512 * (r_ + 1)],
                                         pb[b_][:, 0:512], AF.Identity,
                                         scale=sc(f"D_std{m_}"),
                                         bias=sc(f"F_bc{m_}")))
                mk.bank_done(bk, "es", eh)
            mk.wait("vector", "es", mk.cnt["es"])
            evo = mk.vop(lambda E, m_=mt, c=cur:
                         E.scalar_tensor_tensor(outb[c], ffn_mt(m_),
                                                sc(f"F_rstd{m_}"), outb[c],
                                                op0=ALU.mult, op1=ALU.add),
                         reads=[("ffn", mt), ("outb", cur)],
                         writes=[("outb", cur)])
            mk.wait("gpsimd", "ev", evo)
            r0 = q * QT + 128 * mt
            stv = mk.dma("gpsimd", "st", OUT[r0:r0 + 128, :], outb[cur])
            set_guard(("outb", cur), "st", stv)
        for i in range(5):
            set_guard(("xb", i), "es", mk.cnt["es"])

    mk.wait("gpsimd", "st", mk.cnt["st"])

    with (nc.semaphore("s_ld") as s_ld, nc.semaphore("s_lg") as s_lg,
          nc.semaphore("s_st") as s_st, nc.semaphore("s_mm") as s_mm,
          nc.semaphore("s_ev") as s_ev, nc.semaphore("s_es") as s_es,
          nc.Block() as block):
        mk.sems.update({"ld": s_ld, "lg": s_lg, "st": s_st,
                        "mm": s_mm, "ev": s_ev, "es": s_es})

        @block.sync
        def _(E):
            for fn in mk.ops["sync"]:
                fn(E)

        @block.gpsimd
        def _(E):
            for fn in mk.ops["gpsimd"]:
                fn(E)

        @block.tensor
        def _(E):
            for fn in mk.ops["tensor"]:
                fn(E)

        @block.vector
        def _(E):
            for fn in mk.ops["vector"]:
                fn(E)

        @block.scalar
        def _(E):
            for fn in mk.ops["scalar"]:
                fn(E)
    return nc, mk


def kernel(**inp):
    w = _prepack(inp)
    x = np.asarray(inp["L0_attn_x"], np.float32)
    nc, mk = build()
    in_maps = []
    for i in range(8):
        m = dict(w)
        m["x"] = np.ascontiguousarray(x[i])
        in_maps.append(m)
    res = run_bass_kernel_spmd(nc, in_maps, core_ids=list(range(8)))
    return np.stack([res.results[i]["out"] for i in range(8)], axis=0)


# version 06
# speedup vs baseline: 1.0340x; 1.0158x over previous
import numpy as np
import concourse.bass as bass
import concourse.mybir as mybir
from concourse.bass_utils import run_bass_kernel_spmd

F32 = mybir.dt.float32
AF = mybir.ActivationFunctionType
ALU = mybir.AluOpType
AX = mybir.AxisListType

D = 2048
NTOK = 2048
NQ = 4
QT = 512
QH = 528
NHP = 16
HD = 64
EPS = 1e-5
NEG = -1e30
EXPERTS = 16
MOE_SCALE = 2.5

ENGINES = ("sync", "gpsimd", "tensor", "vector", "scalar")


def _prepack(inp):
    f = np.float32
    qkv_w = np.asarray(inp["L0_attn_qkv_w"], f)
    proj_w = np.asarray(inp["L0_attn_proj_w"], f)
    proj_b = np.asarray(inp["L0_attn_proj_b"], f)
    sw1 = np.asarray(inp["L0_ffn_sw1"], f)
    swg = np.asarray(inp["L0_ffn_swg"], f)
    sw2 = np.asarray(inp["L0_ffn_sw2"], f)
    rw1 = np.asarray(inp["L0_ffn_rw1"], f)
    rwg = np.asarray(inp["L0_ffn_rwg"], f)
    rw2 = np.asarray(inp["L0_ffn_rw2"], f)
    Wg = np.asarray(inp["L0_ffn_Wg"], f)
    Wdown = np.asarray(inp["L0_ffn_Wdown"], f)
    Wup = np.asarray(inp["L0_ffn_Wup"], f)
    biases = np.asarray(inp["L0_ffn_biases"], f)

    w = {}
    t = qkv_w.reshape(3, 16, 128, 16, 128)
    w["qkvw"] = np.ascontiguousarray(t.transpose(1, 0, 4, 3, 2))
    t = proj_w.reshape(16, 128, 16, 128)
    w["pj"] = np.ascontiguousarray(t.transpose(0, 3, 2, 1))
    w["pjb"] = np.ascontiguousarray(proj_b.reshape(1, 2048))
    a = sw1.reshape(8, 128, 16, 128).transpose(0, 3, 2, 1)
    b = swg.reshape(8, 128, 16, 128).transpose(0, 3, 2, 1)
    w["sw1g"] = np.ascontiguousarray(np.stack([a, b], axis=1))
    w["wdn"] = np.ascontiguousarray(
        Wdown.reshape(4, 128, 16, 128).transpose(0, 3, 2, 1))
    w["wg"] = np.ascontiguousarray(Wg.reshape(16, 16, 128).transpose(2, 1, 0))
    a = rw1.reshape(16, 8, 128, 4, 128).transpose(0, 1, 4, 3, 2)
    b = rwg.reshape(16, 8, 128, 4, 128).transpose(0, 1, 4, 3, 2)
    w["rw1g"] = np.ascontiguousarray(np.stack([a, b], axis=2))
    w["rw2"] = np.ascontiguousarray(
        rw2.reshape(16, 512, 8, 128).transpose(0, 3, 2, 1))
    w["sw2"] = np.ascontiguousarray(
        sw2.reshape(8, 256, 8, 128).transpose(0, 3, 2, 1))
    w["wup"] = np.ascontiguousarray(
        Wup.reshape(8, 256, 4, 128).transpose(0, 3, 2, 1))
    w["bias_bc"] = np.ascontiguousarray(np.broadcast_to(biases, (128, 16)))

    p_i = np.arange(128)[:, None]
    c_i = np.arange(143)[None, :]
    o = c_i - p_i - 8
    band = (o >= -8) & (o < 8)
    mval = (8.0 - np.abs(o) / 2.0) / 8.0
    m2, bb = [], []
    for v in range(3):
        valid = band.copy()
        if v == 0:
            valid &= (c_i - 8 >= 0)
        if v == 2:
            valid &= (c_i - 8 < 128)
        m2.append(np.where(valid, mval, 1.0).astype(f))
        bb.append(np.where(valid, 0.0, NEG).astype(f))
    w["maskM"] = np.ascontiguousarray(np.stack(m2, axis=1))
    w["maskB"] = np.ascontiguousarray(np.stack(bb, axis=1))
    w["ident"] = np.eye(128, dtype=f)
    w["onesrow"] = np.ones((1, 128), dtype=f)
    w["consts"] = np.ascontiguousarray(
        np.broadcast_to(np.array([-1.0, EPS], f), (128, 2)))
    return w


from contextlib import ExitStack


class MK:
    def __init__(self, nc, taps=()):
        self.nc = nc
        self.taps = set(taps)
        self.dbg = {}
        self.ops = {e: [] for e in ENGINES}
        self.cnt = {s: 0 for s in ("mm", "ev", "es")}
        self.waited = {}
        self.sems = {}
        self.vw = set()
        self.vr = set()
        self.bankg = [None] * 8

    def emit(self, eng, fn):
        self.ops[eng].append(fn)

    def wait(self, eng, sem, val):
        if not val or val <= 0:
            return
        k = (eng, sem)
        if self.waited.get(k, 0) >= val:
            return
        self.waited[k] = val
        self.emit(eng, lambda E, s=sem, v=val: E.wait_ge(self.sems[s], v))

    def dma(self, eng, sem, out_ap, in_ap):
        self.cnt.setdefault(sem, 0)
        self.cnt[sem] += 16
        v = self.cnt[sem]
        self.emit(eng, lambda E, o=out_ap, i=in_ap, s=sem:
                  E.dma_start(out=o, in_=i).then_inc(self.sems[s], 16))
        return v

    def top(self, fn, inc=True):
        if inc:
            self.cnt["mm"] += 1
            self.emit("tensor", lambda E, f=fn: f(E).then_inc(self.sems["mm"], 1))
            return self.cnt["mm"]
        self.emit("tensor", fn)
        return None

    def vop(self, fn, reads=(), writes=(), inc=True):
        r, w = set(reads), set(writes)
        if (r & self.vw) or (w & self.vr) or (w & self.vw):
            self.emit("vector", lambda E: E.drain())
        self.vr, self.vw = r, w
        if inc:
            self.cnt["ev"] += 1
            self.emit("vector", lambda E, f=fn: f(E).then_inc(self.sems["ev"], 1))
            return self.cnt["ev"]
        self.emit("vector", fn)
        return None

    def sop(self, fn, inc=True):
        if inc:
            self.cnt["es"] += 1
            self.emit("scalar", lambda E, f=fn: f(E).then_inc(self.sems["es"], 1))
            return self.cnt["es"]
        self.emit("scalar", fn)
        return None

    def use_bank(self, b):
        g = self.bankg[b]
        if g:
            self.wait("tensor", g[0], g[1])

    def bank_done(self, b, sem, val):
        self.bankg[b] = (sem, val)

    def fence(self, eng):
        for s in ("mm", "ev", "es"):
            self.wait(eng, s, self.cnt[s])

    def barrier(self):
        for eng in ("tensor", "vector", "scalar", "sync", "gpsimd"):
            for s, v in list(self.cnt.items()):
                self.wait(eng, s, v)

    def tap(self, name, ap, shape):
        if name not in self.taps:
            return
        d = self.nc.dram_tensor("dbg_" + name, list(shape), F32,
                                kind="ExternalOutput").ap()
        self.dbg[name] = d
        for s, v in list(self.cnt.items()):
            if not s.startswith("tap"):
                self.wait("gpsimd", s, v)
        self.dma("gpsimd", "tap", d, ap)


def build(taps=(), nq=NQ):
    nc = bass.Bass("TRN2", target_bir_lowering=False)
    mk = MK(nc, taps)
    f32 = F32

    X = nc.dram_tensor("x", [NTOK, D], f32, kind="ExternalInput").ap()
    OUT = nc.dram_tensor("out", [NTOK, D], f32, kind="ExternalOutput").ap()
    H = {}
    for n, s in [("qkvw", [16, 3, 128, 16, 128]), ("pj", [16, 128, 16, 128]),
                 ("pjb", [1, 2048]), ("sw1g", [8, 2, 128, 16, 128]),
                 ("wdn", [4, 128, 16, 128]), ("wg", [128, 16, 16]),
                 ("rw1g", [16, 8, 2, 128, 4, 128]), ("rw2", [16, 128, 8, 512]),
                 ("sw2", [8, 128, 8, 256]), ("wup", [8, 128, 4, 256]),
                 ("bias_bc", [128, 16]), ("maskM", [128, 3, 143]),
                 ("maskB", [128, 3, 143]), ("ident", [128, 128]),
                 ("onesrow", [1, 128]), ("consts", [128, 2])]:
        H[n] = nc.dram_tensor(n, s, f32, kind="ExternalInput").ap()

    R1 = nc.alloc_sbuf_tensor("R1", [128, 16 * QH], f32).ap()
    R2 = nc.alloc_sbuf_tensor("R2", [128, 16 * 512], f32).ap()
    R34 = nc.alloc_sbuf_tensor("R34", [128, 4 * 2048], f32).ap()
    R7 = nc.alloc_sbuf_tensor("R7", [128, 12288], f32).ap()
    LNB = nc.alloc_sbuf_tensor("LNB", [128, 2048], f32).ap()
    STT = nc.alloc_sbuf_tensor("STT", [128, 192], f32).ap()
    M8S = nc.alloc_sbuf_tensor("M8S", [128, 32], f32).ap()
    MM2 = nc.alloc_sbuf_tensor("MM2", [128, 3 * 143], f32).ap()
    MB2 = nc.alloc_sbuf_tensor("MB2", [128, 3 * 143], f32).ap()
    IDT = nc.alloc_sbuf_tensor("IDT", [128, 128], f32).ap()
    CST = nc.alloc_sbuf_tensor("CST", [128, 2], f32).ap()
    ONE = nc.alloc_sbuf_tensor("ONE", [1, 128], f32).ap()
    PJB = nc.alloc_sbuf_tensor("PJB", [1, 2048], f32).ap()
    BIA = nc.alloc_sbuf_tensor("BIA", [128, 16], f32).ap()
    WDN = nc.alloc_sbuf_tensor("WDS", [128, 64], f32).ap()
    OUTB = nc.alloc_sbuf_tensor("OUTB", [128, 2 * 2048], f32).ap()
    RW2B = nc.alloc_sbuf_tensor("RW2B", [128, 8 * 512], f32).ap()
    nc.const_aps.register(f32, -1.0, CST[:, 0:1])
    nc.const_aps.register(f32, EPS, CST[:, 1:2])

    pb = [nc.alloc_psum_tensor(f"pb{i}", [128, 512], f32).ap() for i in range(8)]

    def r7(a, b):
        return R7[:, a:b]

    def ln1T(c):
        return R1[:, c * QH:(c + 1) * QH]

    def ffn_mt(mt):
        return R1[:, mt * 2048:(mt + 1) * 2048]

    def aoT(hp):
        return R2[:, hp * 512:(hp + 1) * 512]

    def hidT(mh):
        return R2[:, mh * 512:(mh + 1) * 512]

    def hidET(mh):
        return R2[:, 4096 + mh * 512:4096 + (mh + 1) * 512]

    def projh(mt):
        return R34[:, mt * 2048:(mt + 1) * 2048]

    def ln3T_blk(mt, c):
        return R34[:, mt * 2048 + c * 128:mt * 2048 + (c + 1) * 128]



    st_col = {}

    def sc(name):
        if name not in st_col:
            st_col[name] = len(st_col)
            assert len(st_col) <= 192, name
        i = st_col[name]
        return STT[:, i:i + 1]

    ld0 = 0
    for dst, src in [(MM2, H["maskM"]), (MB2, H["maskB"]), (IDT, H["ident"]),
                     (CST, H["consts"]), (ONE, H["onesrow"]), (PJB, H["pjb"]),
                     (BIA, H["bias_bc"])]:
        ld0 = mk.dma("sync", "c", dst, src)
    for e in ("vector", "scalar", "tensor"):
        mk.wait(e, "c", ld0)

    guard = {}

    def buf_guard(key, eng):
        g = guard.get(key)
        if g:
            mk.wait(eng, g[0], g[1])

    def set_guard(key, sem, val):
        guard[key] = (sem, val)

    def ln_math(pre, cols, widths=None):
        widths = widths or [128] * len(cols)
        for m, w in zip(cols, widths):
            mk.vop(lambda E, m_=m, w_=w:
                   E.tensor_scalar_mul(sc(f"{pre}_mu{m_}")[0:w_],
                                       sc(f"{pre}_s2{m_}")[0:w_], 1.0 / D),
                   reads=[f"{pre}_s2{m}"], writes=[f"{pre}_mu{m}"])
        for m, w in zip(cols, widths):
            mk.vop(lambda E, m_=m, w_=w:
                   E.tensor_scalar_mul(sc(f"{pre}_msq{m_}")[0:w_],
                                       sc(f"{pre}_q2{m_}")[0:w_], 1.0 / D),
                   reads=[f"{pre}_q2{m}"], writes=[f"{pre}_msq{m}"])
        for m, w in zip(cols, widths):
            mk.vop(lambda E, m_=m, w_=w:
                   E.scalar_tensor_tensor(sc(f"{pre}_nv{m_}")[0:w_],
                                          sc(f"{pre}_mu{m_}")[0:w_],
                                          sc(f"{pre}_mu{m_}")[0:w_],
                                          sc(f"{pre}_msq{m_}")[0:w_],
                                          op0=ALU.mult, op1=ALU.subtract),
                   reads=[f"{pre}_mu{m}", f"{pre}_msq{m}"],
                   writes=[f"{pre}_nv{m}"])
        mk.wait("scalar", "ev", mk.cnt["ev"])
        for m, w in zip(cols, widths):
            mk.sop(lambda E, m_=m, w_=w:
                   E.activation(sc(f"{pre}_std{m_}")[0:w_],
                                sc(f"{pre}_nv{m_}")[0:w_],
                                AF.Sqrt, bias=EPS, scale=-1.0))
        mk.wait("vector", "es", mk.cnt["es"])
        for m, w in zip(cols, widths):
            mk.vop(lambda E, m_=m, w_=w:
                   E.reciprocal(sc(f"{pre}_rstd{m_}")[0:w_],
                                sc(f"{pre}_std{m_}")[0:w_]),
                   reads=[f"{pre}_std{m}"], writes=[f"{pre}_rstd{m}"])
        for m, w in zip(cols, widths):
            mk.vop(lambda E, m_=m, w_=w:
                   E.scalar_tensor_tensor(sc(f"{pre}_nmr{m_}")[0:w_],
                                          sc(f"{pre}_mu{m_}")[0:w_],
                                          -1.0, sc(f"{pre}_rstd{m_}")[0:w_],
                                          op0=ALU.mult, op1=ALU.mult),
                   reads=[f"{pre}_mu{m}", f"{pre}_rstd{m}"],
                   writes=[f"{pre}_nmr{m}"])
        return mk.cnt["ev"]

    for q in range(nq):
        g0 = q * QT - 8
        mk.fence("gpsimd")
        mk.wait("tensor", "ev", mk.cnt["ev"])

        xb = [r7(2048 * i, 2048 * (i + 1)) for i in range(5)]
        widths = [128, 128, 128, 128, 16]
        stA = []
        for t in range(5):
            w = widths[t]
            r0 = g0 + 128 * t
            cur = t
            g = guard.get(("xb", cur))
            if g:
                mk.wait("sync", g[0], g[1])
            lo, hi = max(r0, 0), min(r0 + w, NTOK)
            p0 = lo - r0
            if p0 > 0 or hi - r0 < w:
                if g:
                    mk.wait("vector", g[0], g[1])
                mvz = mk.vop(lambda E, b=xb[cur], a=p0, ww=w:
                             (E.memset(b[0:a, :], 0.0) if a > 0 else
                              E.memset(b[0:ww, :], 0.0)),
                             writes=[("xb", cur)])
                mk.wait("sync", "ev", mvz)
            lg = mk.dma("sync", f"xb{t}", xb[cur][p0:p0 + (hi - lo), :], X[lo:hi, :])
            mk.wait("scalar", f"xb{t}", lg)
            mk.wait("vector", f"xb{t}", lg)
            es1 = mk.sop(lambda E, b=xb[cur], w_=w, t_=t:
                         E.activation(LNB[0:w_, :], b[0:w_, :], AF.Square,
                                      accum_out=sc(f"A_q2{t_}")[0:w_]))
            mk.vop(lambda E, b=xb[cur], w_=w, t_=t:
                   E.reduce_sum(sc(f"A_s2{t_}")[0:w_], b[0:w_, :], axis=AX.X),
                   reads=[("xb", cur)], writes=[f"A_s2{t}"])
            stA.append((t, w, cur, es1))
        mk.wait("vector", "es", mk.cnt["es"])
        ln_math("A", [t for t, _, _, _ in stA], [w for _, w, _, _ in stA])
        mk.wait("scalar", "ev", mk.cnt["ev"])
        mm_prev_t = 0
        for t, w, cur, _ in stA:
            mk.wait("scalar", "mm", mm_prev_t)
            esap = mk.sop(lambda E, b=xb[cur], w_=w, t_=t:
                          E.activation(LNB[0:w_, :], b[0:w_, :], AF.Identity,
                                       scale=sc(f"A_rstd{t_}")[0:w_],
                                       bias=sc(f"A_nmr{t_}")[0:w_]))
            set_guard(("xb", cur), "es", esap)
            mk.wait("tensor", "es", esap)
            for rnd in range(4):
                bk = 6 + (rnd % 2)
                mk.use_bank(bk)
                for j in range(4):
                    c = rnd * 4 + j
                    mk.top(lambda E, w_=w, c_=c, j_=j, b_=bk:
                           E.transpose(pb[b_][0:128, j_ * 128:j_ * 128 + w_],
                                       LNB[0:w_, c_ * 128:(c_ + 1) * 128],
                                       IDT[0:w_, 0:w_]),
                           inc=(j == 3))
                mmv = mk.cnt["mm"]
                mk.wait("vector", "mm", mmv)
                evv = None
                for j in range(4):
                    c = rnd * 4 + j
                    evv = mk.vop(lambda E, c_=c, j_=j, b_=bk, w_=w, t_=t:
                                 E.tensor_copy(
                                     R1[:, c_ * QH + 128 * t_:c_ * QH + 128 * t_ + w_],
                                     pb[b_][:, 128 * j_:128 * j_ + w_]),
                                 writes=["ln1T"])
                mk.bank_done(bk, "ev", evv)
            mm_prev_t = mk.cnt["mm"]
        ln1_ready = mk.cnt["ev"]
        mk.tap(f"q{q}_ln1T", R1[:, 0:16 * QH], [128, 16 * QH])
        mk.tap(f"q{q}_stats", STT, [128, 192])

        mk.fence("sync")
        wq = [r7(0, 2048), r7(2048, 4096)]
        qkvT = r7(4096, 4096 + 3 * QH)

        def qv(part):
            return qkvT[:, part * QH:(part + 1) * QH]
        vtok = r7(5680, 6000)
        sct = r7(6000, 6143)
        pex = r7(6144, 6287)
        atT = r7(6288, 6544)
        mk.wait("tensor", "ev", ln1_ready)

        for hp in range(NHP):
            evq = [0, 0, 0]
            for part in range(3):
                cur = (3 * hp + part) % 2
                buf_guard(("wq", cur), "sync")
                lw = mk.dma("sync", f"wq{cur}", wq[cur], H["qkvw"][hp, part])
                mk.wait("tensor", f"wq{cur}", lw)
                for sl in range(2):
                    bk = (2 * part + sl) % 4
                    mk.use_bank(bk)
                    for k in range(16):
                        mk.top(lambda E, c=cur, k_=k, s=sl, b_=bk:
                               E.matmul(pb[b_][:, 0:264],
                                        wq[c][:, k_ * 128:(k_ + 1) * 128],
                                        ln1T(k_)[:, s * 264:s * 264 + 264],
                                        start=(k_ == 0), stop=(k_ == 15)),
                               inc=(k == 15))
                    mmv = mk.cnt["mm"]
                    mk.wait("vector", "mm", mmv)
                    if part == 0:
                        evv = mk.vop(lambda E, s=sl, b_=bk:
                                     E.tensor_scalar_mul(
                                         qkvT[:, s * 264:s * 264 + 264],
                                         pb[b_][:, 0:264], HD ** -0.5),
                                     writes=["qkvT"])
                    else:
                        evv = mk.vop(lambda E, p_=part, s=sl, b_=bk:
                                     E.tensor_copy(qv(p_)[:, s * 264:s * 264 + 264],
                                                   pb[b_][:, 0:264]),
                                     writes=["qkvT"])
                    mk.bank_done(bk, "ev", evv)
                set_guard(("wq", cur), "mm", mk.cnt["mm"])
                evq[part] = mk.cnt["ev"]
            if hp == 0:
                mk.tap(f"q{q}_qkvT", qkvT, [128, 3 * QH])

            for h01 in range(2):
                hb = 64 * h01
                mk.wait("tensor", "ev", evq[2])
                mk.use_bank(4)
                for v5 in range(5):
                    wv = 128 if v5 < 4 else 16
                    mk.top(lambda E, v_=v5, w_=wv, h_=hb:
                           E.transpose(pb[4][0:w_, v_ * 64:v_ * 64 + 64],
                                       qv(2)[h_:h_ + 64, v_ * 128:v_ * 128 + w_],
                                       IDT[h_:h_ + 64, h_:h_ + 64]),
                           inc=(v5 == 4))
                mmv = mk.cnt["mm"]
                mk.wait("vector", "mm", mmv)
                ev_vt = mk.vop(lambda E: E.tensor_copy(vtok, pb[4][:, 0:320]),
                               writes=["vtok"])
                mk.bank_done(4, "ev", ev_vt)

                def emit_qk(b):
                    gq = 4 * q + b
                    var = 0 if gq == 0 else (2 if gq == 15 else 1)
                    bk = 4 + (b % 2)
                    mk.use_bank(bk)
                    mk.top(lambda E, b_=b, h_=hb, k_=bk:
                           E.matmul(pb[k_][:, 0:143],
                                    qv(0)[h_:h_ + 64, 8 + 128 * b_:136 + 128 * b_],
                                    qv(1)[h_:h_ + 64, 128 * b_:128 * b_ + 143],
                                    start=True, stop=False), inc=False)
                    mmv = mk.top(lambda E, v_=var, k_=bk:
                                 E.matmul(pb[k_][:, 0:143], IDT,
                                          MB2[:, v_ * 143:(v_ + 1) * 143],
                                          start=False, stop=True))
                    return b, var, bk, mmv

                def emit_tail(u):
                    b, var, bk, mmqk = u
                    mk.wait("vector", "mm", mmqk)
                    mk.vop(lambda E, k_=bk:
                           E.reduce_max(sc("B_nrm"), pb[k_][:, 0:143],
                                        axis=AX.X, negate=True),
                           writes=["B_nrm"])
                    ev_sc = mk.vop(lambda E, v_=var, k_=bk:
                                   E.tensor_tensor(sct, pb[k_][:, 0:143],
                                                   MM2[:, v_ * 143:(v_ + 1) * 143],
                                                   op=ALU.mult),
                                   writes=["sct"])
                    mk.bank_done(bk, "ev", ev_sc)
                    mk.wait("scalar", "ev", ev_sc)
                    es_e = mk.sop(lambda E:
                                  E.activation(pex, sct, AF.Exp, bias=sc("B_nrm"),
                                               accum_out=sc("B_rsum")))
                    mk.wait("vector", "es", es_e)
                    ev_ri = mk.vop(lambda E:
                                   E.reciprocal(sc("B_rinv"), sc("B_rsum")),
                                   reads=["B_rsum"], writes=["B_rinv"])
                    mk.wait("scalar", "ev", ev_ri)
                    es_pn = mk.sop(lambda E: E.mul(sct, pex, sc("B_rinv")))
                    mk.wait("tensor", "es", es_pn)
                    mk.use_bank(6)
                    mk.top(lambda E: E.transpose(pb[6][0:128, 0:128],
                                                 sct[:, 0:128], IDT), inc=False)
                    mmT = mk.top(lambda E: E.transpose(pb[6][0:15, 128:256],
                                                       sct[:, 128:143], IDT))
                    mk.wait("vector", "mm", mmT)
                    ev_aT = mk.vop(lambda E: E.tensor_copy(atT, pb[6][:, 0:256]),
                                   writes=["atT"])
                    mk.bank_done(6, "ev", ev_aT)
                    mk.wait("tensor", "ev", ev_aT)
                    mk.use_bank(7)
                    mk.top(lambda E, b_=b:
                           E.matmul(pb[7][0:64, 0:128],
                                    vtok[:, 64 * b_:64 * b_ + 64],
                                    atT[:, 0:128], start=True, stop=False),
                           inc=False)
                    mmAV = mk.top(lambda E, b_=b:
                                  E.matmul(pb[7][0:64, 0:128],
                                           vtok[0:15, 64 * b_ + 64:64 * b_ + 128],
                                           atT[0:15, 128:256],
                                           start=False, stop=True))
                    mk.wait("vector", "mm", mmAV)
                    ev_o = mk.vop(lambda E, b_=b, h_=hb, hp_=hp:
                                  E.tensor_copy(
                                      aoT(hp_)[h_:h_ + 64, 128 * b_:128 * b_ + 128],
                                      pb[7][0:64, 0:128]),
                                  writes=["aoT"])
                    mk.bank_done(7, "ev", ev_o)

                prev = None
                for b in range(4):
                    u = emit_qk(b)
                    if prev is not None:
                        emit_tail(prev)
                    prev = u
                emit_tail(prev)
        ao_ready = mk.cnt["ev"]
        mk.tap(f"q{q}_aoT", R2[:, 0:8192], [128, 8192])

        mk.fence("sync")
        mk.fence("gpsimd")
        wpj = [r7(0, 2048), r7(2048, 4096)]
        xr = [r7(4096, 6144), r7(6144, 8192)]
        mk.wait("tensor", "ev", ao_ready)
        cps_guard = {}
        for m in range(4):
            mk.vop(lambda E, m_=m: E.memset(sc(f"C_s2{m_}"), 0.0),
                   writes=[f"C_s2{m}"], inc=False)
            mk.vop(lambda E, m_=m: E.memset(sc(f"C_q2{m_}"), 0.0),
                   writes=[f"C_q2{m}"], inc=False)
        for n16 in range(16):
            cur = n16 % 2
            buf_guard(("wpj", cur), "sync")
            lw = mk.dma("sync", f"wpj{cur}", wpj[cur], H["pj"][n16])
            mk.wait("tensor", f"wpj{cur}", lw)
            for m in range(4):
                bk = m % 4
                mk.use_bank(bk)
                for k in range(16):
                    mk.top(lambda E, c=cur, k_=k, m_=m, b_=bk:
                           E.matmul(pb[b_][:, 0:128],
                                    aoT(k_)[:, 128 * m_:128 * m_ + 128],
                                    wpj[c][:, 128 * k_:128 * k_ + 128],
                                    start=(k_ == 0), stop=False), inc=False)
                mmv = mk.top(lambda E, n_=n16, b_=bk:
                             E.matmul(pb[b_][:, 0:128], ONE,
                                      PJB[0:1, 128 * n_:128 * n_ + 128],
                                      start=False, stop=True))
                mk.wait("scalar", "mm", mmv)
                par = n16 % 2
                mk.wait("scalar", "ev", cps_guard.get((m, par), 0))
                e1 = mk.sop(lambda E, m_=m, n_=n16, b_=bk, p_=par:
                            E.activation(projh(m_)[:, 128 * n_:128 * n_ + 128],
                                         pb[b_][:, 0:128], AF.Identity,
                                         accum_out=sc(f"C_ps{m_}_{p_}")))
                e2 = mk.sop(lambda E, b_=bk, p_=par, m_=m:
                            E.activation(LNB[:, 0:128], pb[b_][:, 0:128],
                                         AF.Square, accum_out=sc(f"C_pq{m_}_{p_}")))
                mk.bank_done(bk, "es", e2)
                mk.wait("vector", "es", e2)
                mk.vop(lambda E, m_=m, p_=par:
                       E.tensor_add(sc(f"C_s2{m_}"), sc(f"C_s2{m_}"), sc(f"C_ps{m_}_{p_}")),
                       reads=[f"C_ps{m}_{par}", f"C_s2{m}"], writes=[f"C_s2{m}"])
                ev2 = mk.vop(lambda E, m_=m, p_=par:
                             E.tensor_add(sc(f"C_q2{m_}"), sc(f"C_q2{m_}"), sc(f"C_pq{m_}_{p_}")),
                             reads=[f"C_pq{m}_{par}", f"C_q2{m}"], writes=[f"C_q2{m}"])
                cps_guard[(m, par)] = ev2
            set_guard(("wpj", cur), "mm", mk.cnt["mm"])
        mk.tap(f"q{q}_proj", R34[:, 0:8192], [128, 8192])

        ln_math("C", range(4))
        mk.wait("scalar", "ev", mk.cnt["ev"])
        for m in range(4):
            cur = m % 2
            buf_guard(("xr", cur), "sync")
            r0 = q * QT + 128 * m
            lg = mk.dma("sync", f"xr{cur}", xr[cur], X[r0:r0 + 128, :])
            mk.wait("scalar", f"xr{cur}", lg)
            mk.wait("scalar", "ev", mk.cnt["ev"])
            e1 = mk.sop(lambda E, c=cur, m_=m:
                        E.activation(LNB, xr[c], AF.Identity, bias=sc(f"C_nmr{m_}")))
            set_guard(("xr", cur), "es", e1)
            mk.wait("vector", "es", e1)
            evh = mk.vop(lambda E, m_=m:
                         E.scalar_tensor_tensor(projh(m_), projh(m_),
                                                sc(f"C_rstd{m_}"), LNB,
                                                op0=ALU.mult, op1=ALU.add),
                         reads=["LNB", ("projh", m)], writes=[("projh", m)])
            mk.wait("scalar", "ev", evh)
            mk.sop(lambda E, m_=m:
                   E.activation(LNB, projh(m_), AF.Square,
                                accum_out=sc(f"D_q2{m_}")))
            mk.vop(lambda E, m_=m:
                   E.reduce_sum(sc(f"D_s2{m_}"), projh(m_), axis=AX.X),
                   reads=[("projh", m)], writes=[f"D_s2{m}"])
        mk.wait("vector", "es", mk.cnt["es"])
        ln_math("D", range(4))
        mk.wait("scalar", "ev", mk.cnt["ev"])
        mm_prev_t = 0
        for m in range(4):
            mk.wait("scalar", "mm", mm_prev_t)
            e1 = mk.sop(lambda E, m_=m:
                        E.activation(LNB, projh(m_), AF.Identity,
                                     scale=sc(f"D_rstd{m_}"), bias=sc(f"D_nmr{m_}")))
            mk.wait("tensor", "es", e1)
            for rnd in range(4):
                bk = 6 + (rnd % 2)
                mk.use_bank(bk)
                for j in range(4):
                    c = rnd * 4 + j
                    mk.top(lambda E, c_=c, j_=j, b_=bk:
                           E.transpose(pb[b_][0:128, j_ * 128:(j_ + 1) * 128],
                                       LNB[:, c_ * 128:(c_ + 1) * 128], IDT),
                           inc=(j == 3))
                mmv = mk.cnt["mm"]
                mk.wait("vector", "mm", mmv)
                evv = mk.vop(lambda E, m_=m, b_=bk, r_=rnd:
                             E.tensor_copy(
                                 R34[:, m_ * 2048 + r_ * 512:m_ * 2048 + (r_ + 1) * 512],
                                 pb[b_][:, 0:512]),
                             writes=["ln3T"])
                mk.bank_done(bk, "ev", evv)
            mm_prev_t = mk.cnt["mm"]
        ln3_ready = mk.cnt["ev"]
        mk.tap(f"q{q}_ln3T", R34[:, 0:8192], [128, 8192])

        mk.fence("sync")
        ws = [r7(0, 2048), r7(2048, 4096)]
        xlT = r7(4096, 6144)
        accTok = r7(6144, 8192)
        accT = r7(8192, 10240)
        wgb = r7(10240, 10496)
        prT = r7(10496, 11008)
        moe = r7(11008, 11264)
        mk.wait("tensor", "ev", ln3_ready)
        wcur = [0]

        def wslot():
            s = wcur[0] % 2
            wcur[0] += 1
            return s

        prev_evv = 0
        for mh in range(8):
            psl = {}
            for t2 in (1, 0):
                s = wslot()
                buf_guard(("ws", s), "sync")
                lw = mk.dma("sync", f"ws{s}", ws[s], H["sw1g"][mh, t2])
                mk.wait("tensor", f"ws{s}", lw)
                bk = (2 * mh + t2) % 4
                mk.use_bank(bk)
                for k in range(16):
                    for mt4 in range(4):
                        mk.top(lambda E, s_=s, k_=k, b_=bk, m_=mt4:
                               E.matmul(pb[b_][:, 128 * m_:128 * m_ + 128],
                                        ws[s_][:, 128 * k_:128 * k_ + 128],
                                        ln3T_blk(m_, k_),
                                        start=(k_ == 0 and m_ == 0),
                                        stop=(k_ == 15 and m_ == 3)),
                               inc=(k == 15 and mt4 == 3))
                psl[t2] = (bk, mk.cnt["mm"])
                set_guard(("ws", s), "mm", mk.cnt["mm"])
            bkg, mmg = psl[1]
            bkl, mml = psl[0]
            mk.wait("scalar", "mm", mmg)
            mk.wait("scalar", "ev", prev_evv)
            e1 = mk.sop(lambda E, b_=bkg: E.activation(LNB[:, 0:512], pb[b_], AF.Relu))
            e2 = mk.sop(lambda E: E.activation(LNB[:, 512:1024], LNB[:, 0:512],
                                               AF.Square))
            mk.bank_done(bkg, "es", e1)
            mk.wait("vector", "es", e2)
            mk.wait("vector", "mm", mml)
            evv = mk.vop(lambda E, m_=mh, b_=bkl:
                         E.tensor_tensor(hidT(m_), LNB[:, 512:1024], pb[b_],
                                         op=ALU.mult),
                         reads=["LNB2"], writes=[("hidT", mh)])
            mk.bank_done(bkl, "ev", evv)
            prev_evv = evv
        for ml in range(4):
            s = wslot()
            buf_guard(("ws", s), "sync")
            lw = mk.dma("sync", f"ws{s}", ws[s], H["wdn"][ml])
            mk.wait("tensor", f"ws{s}", lw)
            bk = ml % 4
            mk.use_bank(bk)
            for k in range(16):
                for mt4 in range(4):
                    mk.top(lambda E, s_=s, k_=k, b_=bk, m_=mt4:
                           E.matmul(pb[b_][:, 128 * m_:128 * m_ + 128],
                                    ws[s_][:, 128 * k_:128 * k_ + 128],
                                    ln3T_blk(m_, k_),
                                    start=(k_ == 0 and m_ == 0),
                                    stop=(k_ == 15 and m_ == 3)),
                           inc=(k == 15 and mt4 == 3))
            mmv = mk.cnt["mm"]
            set_guard(("ws", s), "mm", mmv)
            mk.wait("vector", "mm", mmv)
            evv = mk.vop(lambda E, m_=ml, b_=bk:
                         E.tensor_copy(xlT[:, 512 * m_:512 * m_ + 512], pb[b_]),
                         writes=["xlT"])
            mk.bank_done(bk, "ev", evv)
        mk.tap(f"q{q}_xlT", xlT, [128, 2048])
        lw = mk.dma("sync", "wgb", wgb, H["wg"])
        mk.wait("tensor", "wgb", lw)
        mk.use_bank(4)
        for k in range(16):
            for mt4 in range(4):
                mk.top(lambda E, k_=k, m_=mt4:
                       E.matmul(pb[4][0:16, 128 * m_:128 * m_ + 128],
                                wgb[:, 16 * k_:16 * k_ + 16],
                                ln3T_blk(m_, k_),
                                start=(k_ == 0 and m_ == 0),
                                stop=(k_ == 15 and m_ == 3)),
                       inc=(k == 15 and mt4 == 3))
        mmv = mk.cnt["mm"]
        mk.wait("vector", "mm", mmv)
        ev_pr = mk.vop(lambda E: E.tensor_copy(prT[0:16, :], pb[4][0:16, 0:512]),
                       writes=["prT"])
        mk.bank_done(4, "ev", ev_pr)
        mk.wait("tensor", "ev", ev_pr)
        mk.use_bank(5)
        for mt in range(4):
            mk.top(lambda E, m_=mt:
                   E.transpose(pb[5][0:128, 16 * m_:16 * m_ + 16],
                               prT[0:16, 128 * m_:128 * m_ + 128],
                               IDT[0:16, 0:16]),
                   inc=(mt == 3))
        mmv = mk.cnt["mm"]
        mk.wait("scalar", "mm", mmv)
        es_sig = mk.sop(lambda E: E.activation(moe[:, 0:64], pb[5][:, 0:64],
                                               AF.Sigmoid))
        mk.bank_done(5, "es", es_sig)
        mk.wait("vector", "es", es_sig)

        def sig_mt(mt):
            return moe[:, 16 * mt:16 * mt + 16]

        def z_mt(mt):
            return moe[:, 64 + 16 * mt:80 + 16 * mt]

        def ch_mt(mt):
            return moe[:, 128 + 16 * mt:144 + 16 * mt]

        for mt in range(4):
            mk.vop(lambda E, m_=mt: E.tensor_add(z_mt(m_), sig_mt(m_), BIA),
                   reads=[("sig", mt)], writes=[("zsc", mt)])
        for mt in range(4):
            mk.vop(lambda E, m_=mt: E.max(M8S[:, 8 * m_:8 * m_ + 8], z_mt(m_)),
                   reads=[("zsc", mt)], writes=[("m8", mt)])
        for mt in range(4):
            mk.vop(lambda E, m_=mt:
                   E.match_replace(ch_mt(m_), in_to_replace=M8S[:, 8 * m_:8 * m_ + 8],
                                   in_values=z_mt(m_), imm_value=NEG),
                   reads=[("m8", mt), ("zsc", mt)], writes=[("chz", mt)])
        for mt in range(4):
            mk.vop(lambda E, m_=mt:
                   E.tensor_tensor(z_mt(m_), z_mt(m_), ch_mt(m_), op=ALU.not_equal),
                   reads=[("chz", mt), ("zsc", mt)], writes=[("msk", mt)])
        for mt in range(4):
            mk.vop(lambda E, m_=mt:
                   E.tensor_tensor(ch_mt(m_), sig_mt(m_), z_mt(m_), op=ALU.mult),
                   reads=[("msk", mt), ("sig", mt)], writes=[("chp", mt)])
        for mt in range(4):
            mk.vop(lambda E, m_=mt:
                   E.reduce_sum(sc(f"Mden{m_}"), ch_mt(m_), axis=AX.X),
                   reads=[("chp", mt)], writes=[f"Mden{mt}"])
        for mt in range(4):
            mk.vop(lambda E, m_=mt:
                   E.reciprocal(sc(f"Mri{m_}"), sc(f"Mden{m_}")),
                   reads=[f"Mden{mt}"], writes=[f"Mri{mt}"])
        for mt in range(4):
            mk.vop(lambda E, m_=mt:
                   E.tensor_scalar(WDN[:, 16 * m_:16 * m_ + 16], ch_mt(m_),
                                   sc(f"Mri{m_}"), MOE_SCALE,
                                   op0=ALU.mult, op1=ALU.mult),
                   reads=[("chp", mt), f"Mri{mt}"], writes=[("wd", mt)])
        mk.tap(f"q{q}_wd", WDN, [128, 64])

        mk.vop(lambda E: E.memset(accTok, 0.0), writes=["accTok_all"])
        for e in range(EXPERTS):
            prev_evv = 0
            for mh in range(8):
                psl = {}
                for t2 in (1, 0):
                    s = wslot()
                    buf_guard(("ws", s), "sync")
                    lw = mk.dma("sync", f"ws{s}", ws[s][:, 0:512], H["rw1g"][e, mh, t2])
                    mk.wait("tensor", f"ws{s}", lw)
                    bk = (2 * mh + t2) % 4
                    mk.use_bank(bk)
                    for k in range(4):
                        mk.top(lambda E, s_=s, k_=k, b_=bk:
                               E.matmul(pb[b_][:, 0:512],
                                        ws[s_][:, 128 * k_:128 * k_ + 128],
                                        xlT[:, 512 * k_:512 * k_ + 512],
                                        start=(k_ == 0), stop=(k_ == 3)),
                               inc=(k == 3))
                    psl[t2] = (bk, mk.cnt["mm"])
                    set_guard(("ws", s), "mm", mk.cnt["mm"])
                bkg, mmg = psl[1]
                bkl, mml = psl[0]
                mk.wait("scalar", "mm", mmg)
                mk.wait("scalar", "ev", prev_evv)
                e1 = mk.sop(lambda E, b_=bkg: E.activation(LNB[:, 0:512], pb[b_],
                                                           AF.Relu))
                e2 = mk.sop(lambda E: E.activation(LNB[:, 512:1024], LNB[:, 0:512],
                                                   AF.Square))
                mk.bank_done(bkg, "es", e1)
                mk.wait("vector", "es", e2)
                mk.wait("vector", "mm", mml)
                evv = mk.vop(lambda E, m_=mh, b_=bkl:
                             E.tensor_tensor(hidET(m_), LNB[:, 512:1024], pb[b_],
                                             op=ALU.mult),
                             reads=["LNB2"], writes=[("hidET", mh)])
                mk.bank_done(bkl, "ev", evv)
                prev_evv = evv
            mk.wait("tensor", "ev", prev_evv)
            if e == 0:
                mk.tap(f"q{q}_hidE0", R2[:, 4096:8192], [128, 4096])
            buf_guard(("rw2b", 0), "sync")
            lw = mk.dma("sync", "rw2b", RW2B, H["rw2"][e])
            mk.wait("tensor", "rw2b", lw)
            for mt in range(4):
                mk.use_bank(4 + mt)
                for k in range(8):
                    mk.top(lambda E, k_=k, m_=mt:
                           E.matmul(pb[4 + m_][:, 0:512],
                                    hidET(k_)[:, 128 * m_:128 * m_ + 128],
                                    RW2B[:, 512 * k_:512 * k_ + 512],
                                    start=(k_ == 0), stop=(k_ == 7)),
                           inc=(k == 7))
            set_guard(("rw2b", 0), "mm", mk.cnt["mm"])
            mk.wait("vector", "mm", mk.cnt["mm"])
            for mt in range(4):
                evv = mk.vop(lambda E, m_=mt, e_=e:
                             E.scalar_tensor_tensor(
                                 accTok[:, 512 * m_:512 * m_ + 512],
                                 pb[4 + m_],
                                 WDN[:, 16 * m_ + e_:16 * m_ + e_ + 1],
                                 accTok[:, 512 * m_:512 * m_ + 512],
                                 op0=ALU.mult, op1=ALU.add),
                             reads=[("accTok", mt)], writes=[("accTok", mt)])
                mk.bank_done(4 + mt, "ev", evv)
            if e == 0:
                mk.tap(f"q{q}_accE0", accTok, [128, 2048])
        acc_ready = mk.cnt["ev"]
        mk.tap(f"q{q}_accTok", accTok, [128, 2048])

        mk.wait("tensor", "ev", acc_ready)
        for mt in range(4):
            bk = mt % 2
            mk.use_bank(bk)
            for j in range(4):
                mk.top(lambda E, m_=mt, j_=j, b_=bk:
                       E.transpose(pb[b_][0:128, 128 * j_:128 * j_ + 128],
                                   accTok[:, 512 * m_ + 128 * j_:
                                          512 * m_ + 128 * (j_ + 1)],
                                   IDT), inc=(j == 3))
            mmv = mk.cnt["mm"]
            mk.wait("vector", "mm", mmv)
            evv = None
            for j in range(4):
                evv = mk.vop(lambda E, m_=mt, j_=j, b_=bk:
                             E.tensor_copy(
                                 accT[:, 512 * j_ + 128 * m_:512 * j_ + 128 * m_ + 128],
                                 pb[b_][:, 128 * j_:128 * j_ + 128]),
                             writes=["accT"])
            mk.bank_done(bk, "ev", evv)
        accT_ready = mk.cnt["ev"]
        mk.wait("tensor", "ev", accT_ready)

        fps_guard = {}
        for mt in range(4):
            mk.vop(lambda E, m_=mt: E.memset(sc(f"F_s2{m_}"), 0.0),
                   writes=[f"F_s2{mt}"], inc=False)
            mk.vop(lambda E, m_=mt: E.memset(sc(f"F_q2{m_}"), 0.0),
                   writes=[f"F_q2{mt}"], inc=False)
        for n in range(8):
            s = wslot()
            buf_guard(("ws", s), "sync")
            lw = mk.dma("sync", f"ws{s}", ws[s], H["sw2"][n])
            s2 = wslot()
            buf_guard(("ws", s2), "sync")
            lw2 = mk.dma("sync", f"ws{s2}", ws[s2][:, 0:1024], H["wup"][n])
            mk.wait("tensor", f"ws{s}", lw)
            mk.wait("tensor", f"ws{s2}", lw2)
            for mt in range(4):
                bk = mt % 4
                mk.use_bank(bk)
                for k in range(8):
                    mk.top(lambda E, s_=s, k_=k, m_=mt, b_=bk:
                           E.matmul(pb[b_][:, 0:256],
                                    hidT(k_)[:, 128 * m_:128 * m_ + 128],
                                    ws[s_][:, 256 * k_:256 * k_ + 256],
                                    start=(k_ == 0), stop=False), inc=False)
                for j in range(4):
                    mk.top(lambda E, s_=s2, j_=j, m_=mt, b_=bk:
                           E.matmul(pb[b_][:, 0:256],
                                    accT[:, 512 * j_ + 128 * m_:
                                         512 * j_ + 128 * m_ + 128],
                                    ws[s_][:, 256 * j_:256 * j_ + 256],
                                    start=False, stop=(j_ == 3)),
                           inc=(j == 3))
                mmv = mk.cnt["mm"]
                mk.wait("scalar", "mm", mmv)
                par = n % 2
                mk.wait("scalar", "ev", fps_guard.get((mt, par), 0))
                e1 = mk.sop(lambda E, m_=mt, n_=n, b_=bk, p_=par:
                            E.activation(ffn_mt(m_)[:, 256 * n_:256 * n_ + 256],
                                         pb[b_][:, 0:256], AF.Identity,
                                         accum_out=sc(f"F_ps{m_}_{p_}")))
                e2 = mk.sop(lambda E, b_=bk, p_=par, m_=mt:
                            E.activation(LNB[:, 0:256], pb[b_][:, 0:256],
                                         AF.Square, accum_out=sc(f"F_pq{m_}_{p_}")))
                mk.bank_done(bk, "es", e2)
                mk.wait("vector", "es", e2)
                mk.vop(lambda E, m_=mt, p_=par:
                       E.tensor_add(sc(f"F_s2{m_}"), sc(f"F_s2{m_}"), sc(f"F_ps{m_}_{p_}")),
                       reads=[f"F_ps{mt}_{par}", f"F_s2{mt}"], writes=[f"F_s2{mt}"])
                ev2 = mk.vop(lambda E, m_=mt, p_=par:
                             E.tensor_add(sc(f"F_q2{m_}"), sc(f"F_q2{m_}"), sc(f"F_pq{m_}_{p_}")),
                             reads=[f"F_pq{mt}_{par}", f"F_q2{mt}"], writes=[f"F_q2{mt}"])
                fps_guard[(mt, par)] = ev2
            set_guard(("ws", s), "mm", mk.cnt["mm"])
            set_guard(("ws", s2), "mm", mk.cnt["mm"])
        ln_math("F", range(4))
        for mt in range(4):
            mk.vop(lambda E, m_=mt:
                   E.tensor_add(sc(f"F_bc{m_}"), sc(f"D_mu{m_}"), sc(f"F_nmr{m_}")),
                   reads=[f"F_nmr{mt}"], writes=[f"F_bc{mt}"])
        mk.wait("scalar", "ev", mk.cnt["ev"])
        mk.tap(f"q{q}_ffn", R1[:, 0:8192], [128, 8192])

        outb = [OUTB[:, 0:2048], OUTB[:, 2048:4096]]
        for mt in range(4):
            cur = mt % 2
            buf_guard(("outb", cur), "scalar")
            for rnd in range(4):
                bk = 4 + (rnd % 2)
                mk.use_bank(bk)
                for j in range(4):
                    c = rnd * 4 + j
                    mk.top(lambda E, m_=mt, c_=c, j_=j, b_=bk:
                           E.transpose(pb[b_][0:128, 128 * j_:128 * (j_ + 1)],
                                       ln3T_blk(m_, c_), IDT), inc=(j == 3))
                mmv = mk.cnt["mm"]
                mk.wait("scalar", "mm", mmv)
                eh = mk.sop(lambda E, m_=mt, r_=rnd, c=cur, b_=bk:
                            E.activation(outb[c][:, 512 * r_:512 * (r_ + 1)],
                                         pb[b_][:, 0:512], AF.Identity,
                                         scale=sc(f"D_std{m_}"),
                                         bias=sc(f"F_bc{m_}")))
                mk.bank_done(bk, "es", eh)
            mk.wait("vector", "es", mk.cnt["es"])
            evo = mk.vop(lambda E, m_=mt, c=cur:
                         E.scalar_tensor_tensor(outb[c], ffn_mt(m_),
                                                sc(f"F_rstd{m_}"), outb[c],
                                                op0=ALU.mult, op1=ALU.add),
                         reads=[("ffn", mt), ("outb", cur)],
                         writes=[("outb", cur)])
            mk.wait("gpsimd", "ev", evo)
            r0 = q * QT + 128 * mt
            stv = mk.dma("gpsimd", f"st{cur}", OUT[r0:r0 + 128, :], outb[cur])
            set_guard(("outb", cur), f"st{cur}", stv)
        for i in range(5):
            set_guard(("xb", i), "es", mk.cnt["es"])

    for s in ("st0", "st1", "tap"):
        if s in mk.cnt:
            mk.wait("gpsimd", s, mk.cnt[s])

    with ExitStack() as stack:
        for name in sorted(mk.cnt.keys()):
            mk.sems[name] = stack.enter_context(nc.semaphore("s_" + name))
        block = stack.enter_context(nc.Block())

        @block.sync
        def _(E):
            for fn in mk.ops["sync"]:
                fn(E)

        @block.gpsimd
        def _(E):
            for fn in mk.ops["gpsimd"]:
                fn(E)

        @block.tensor
        def _(E):
            for fn in mk.ops["tensor"]:
                fn(E)

        @block.vector
        def _(E):
            for fn in mk.ops["vector"]:
                fn(E)

        @block.scalar
        def _(E):
            for fn in mk.ops["scalar"]:
                fn(E)
    return nc, mk


def kernel(**inp):
    w = _prepack(inp)
    x = np.asarray(inp["L0_attn_x"], np.float32)
    nc, mk = build()
    in_maps = []
    for i in range(8):
        m = dict(w)
        m["x"] = np.ascontiguousarray(x[i])
        in_maps.append(m)
    res = run_bass_kernel_spmd(nc, in_maps, core_ids=list(range(8)))
    return np.stack([res.results[i]["out"] for i in range(8)], axis=0)
